# version 15
# baseline (speedup 1.0000x reference)
import contextlib
import os
os.environ["TRN_ALLOW_ICI_ALL_TO_ALL"] = "1"
import numpy as np
import ml_dtypes
import concourse.bass as bass
import concourse.mybir as mybir
from concourse.bass_utils import run_bass_kernel_spmd

F32 = mybir.dt.float32
BF16 = mybir.dt.bfloat16
AF = mybir.ActivationFunctionType
ALU = mybir.AluOpType
AX = mybir.AxisListType

D = 4096
S = 8192
NCORE = 8
TOK = 1024
NCH = 32
NH = 32
NKV = 8
HD = 128
SCALE = HD ** -0.5
EXPB = -20.0
EPS = 1e-5
NEG = -1.0e9

C_EXPB, C_SCALE, C_EPS, C_NHALF, C_INVD, C_M1, C_0125, C_2, C_0, C_1 = range(10)
CVALS = [EXPB, SCALE, EPS, -0.5, 1.0 / D, -1.0, 0.125, 2.0, 0.0, 1.0]

DBG = False
PH_LIMIT = 99
QKV_LIMIT = 99
DO_CC = True

HW_ENG = ("pe", "act", "dve", "gp", "sy")
DMA_SEMS = ("d_kb0", "d_kb1", "d_vb0", "d_vb1", "d_cink", "d_cinv",
            "d_small", "d_x0", "d_x1", "d_st", "d_wb0", "d_wb1",
            "d_gpm")
ENG = HW_ENG + ("cc",) + DMA_SEMS
ASYNC = set(DMA_SEMS) | {"cc"}


class Tracker:

    def __init__(self):
        self.cnt = {e: 0 for e in ENG}
        self.waited = {}
        self.bufs = {}
        self.q = {e: [] for e in HW_ENG}
        self.phase = 0

    def _b(self, key):
        if key not in self.bufs:
            self.bufs[key] = {"w": None, "r": []}
        return self.bufs[key]

    def emit(self, eng, fn, reads=(), writes=(), dma=False, sem=None, inc=None,
             manual_waits=None, force=False):
        if self.phase > PH_LIMIT and not force:
            return None
        sem = sem or eng
        assert (sem in ASYNC) == (dma or sem == "cc"), (eng, sem, dma)
        inc = inc if inc is not None else (16 if dma else 1)
        waits = {}

        def add_wait(dep):
            if dep is None:
                return
            deng, dval = dep
            if deng == eng and eng == "pe":
                return
            if deng in ASYNC and deng != "cc":
                dval = self.cnt[deng]
            if dval <= self.waited.get((eng, deng), 0):
                return
            waits[deng] = max(waits.get(deng, 0), dval)

        for k in reads:
            add_wait(self._b(k)["w"])
        for k in writes:
            b = self._b(k)
            add_wait(b["w"])
            for r in b["r"]:
                add_wait(r)
        if manual_waits:
            for deng, dval in manual_waits.items():
                add_wait((deng, dval))
        self.cnt[sem] += inc
        me = (sem, self.cnt[sem]) if inc else None
        if me is not None:
            for k in reads:
                self._b(k)["r"].append(me)
            for k in writes:
                b = self._b(k)
                b["w"] = me
                b["r"] = []
        for deng, dval in waits.items():
            self.waited[(eng, deng)] = dval
        self.q[eng].append((dict(waits), fn, sem, inc))
        return me

    def barrier(self):
        self.phase += 1
        if self.phase > PH_LIMIT:
            return
        snap = dict(self.cnt)
        for e in HW_ENG:
            mw = {d: snap[d] for d in ENG
                  if snap[d] > 0 and not (d == e and d not in ASYNC)}
            self.emit(e, lambda en: en.nop(), manual_waits=mw, inc=0)
        for b in self.bufs.values():
            b["w"] = None
            b["r"] = []

    def replay(self, eng_name, eng_obj, sems):
        for waits, fn, sem, inc in self.q[eng_name]:
            for deng, dval in waits.items():
                eng_obj.wait_ge(sems[deng], dval)
            inst = fn(eng_obj)
            if inst is not None and inc:
                inst.then_inc(sems[sem], inc)


def build_program():
    nc = bass.Bass("TRN2", target_bir_lowering=False)

    def din(name, shape, dt=F32):
        return nc.dram_tensor(name, shape, dt, kind="ExternalInput").ap()

    xT = din("xT", [D, TOK])
    wqp = din("wqp", [128, NCH * NCH * 128])
    wkp = din("wkp", [128, NKV * NCH * 128])
    wvp = din("wvp", [128, NKV * NCH * 128])
    wop = din("wop", [128, NCH * NCH * 128])
    wbp = din("wbp", [128, NCH * NCH * 128])
    wa_p = din("wa_p", [128, NCH * 128])
    wgt_p = din("wgt_p", [128, NCH * 9])
    wbw_p = din("wbw_p", [128, D])
    bb_p = din("bb_p", [128, NCH])
    bthr_bc = din("bthr_bc", [128, 1])
    eye_b = din("eye_b", [128, 128], BF16)
    eye_f = din("eye_f", [128, 128])
    ones_b = din("ones_b", [128, 512], BF16)
    onesf = din("onesf", [128, 128])
    emat = din("emat", [8, 128])
    tri_in = din("tri_in", [128, 4 * 512])

    out = nc.dram_tensor("out", [TOK, D], F32, kind="ExternalOutput").ap()

    cin_k = nc.dram_tensor("cin_k", [NKV * HD, TOK], BF16, kind="Internal")
    ata_k = nc.dram_tensor("ata_k", [NKV * HD, TOK], BF16, kind="Internal")
    cin_v3 = nc.dram_tensor("cin_v3", [NKV * 128, 8 * HD], BF16, kind="Internal")
    ata_v = nc.dram_tensor("ata_v", [NKV * 128, 8 * HD], BF16, kind="Internal")
    cin_qA = nc.dram_tensor("cin_qA", [NH * HD // 2, TOK], BF16, kind="Internal")
    cin_qB = nc.dram_tensor("cin_qB", [NH * HD // 2, TOK], BF16, kind="Internal")
    ata_qA = nc.dram_tensor("ata_qA", [NH * HD // 2, TOK], BF16, kind="Internal")
    ata_qB = nc.dram_tensor("ata_qB", [NH * HD // 2, TOK], BF16, kind="Internal")
    cin_aoA = nc.dram_tensor("cin_aoA", [NH * HD // 2, TOK], BF16, kind="Internal")
    cin_aoB = nc.dram_tensor("cin_aoB", [NH * HD // 2, TOK], BF16, kind="Internal")
    ata_aoA = nc.dram_tensor("ata_aoA", [NH * HD // 2, TOK], BF16, kind="Internal")
    ata_aoB = nc.dram_tensor("ata_aoB", [NH * HD // 2, TOK], BF16, kind="Internal")

    es = contextlib.ExitStack()
    T = Tracker()

    with es:
        def sb(name, shape, dt):
            return es.enter_context(nc.sbuf_tensor(name, shape, dt))

        def psa(name, shape, dt=F32):
            return es.enter_context(nc.psum_tensor(name, shape, dt))

        BIG1 = sb("BIG1", [128, NCH, TOK], BF16)
        BIG2 = sb("BIG2", [128, NCH, TOK], BF16)
        RK = sb("RK", [128, 8192], BF16)
        RV = sb("RV", [128, 8192], BF16)
        WB = sb("WB", [128, 8192], BF16)
        F1 = sb("F1", [128, 7168], F32)
        cc = sb("cc", [128, 16], F32)
        eyeb_s = sb("eyeb_s", [128, 128], BF16)
        eyef_s = sb("eyef_s", [128, 128], F32)
        onesb_s = sb("onesb_s", [128, 512], BF16)
        onesf_s = sb("onesf_s", [128, 128], F32)
        emat_s = sb("emat_s", [8, 128], F32)
        wgt_s = sb("wgt_s", [128, NCH * 9], BF16)
        bb_s = sb("bb_s", [128, NCH], F32)
        bthr_s = sb("bthr_s", [128, 1], F32)

        kbuf = [RK[:, 0:4096], RK[:, 4096:8192]]
        wa_v = RK[:, 0:4096].rearrange("p (k j) -> p k j", j=128)
        wbw_v = RK[:, 4096:8192].rearrange("p (k j) -> p k j", j=128)
        vb_flat = [RV[:, 0:4096], RV[:, 4096:8192]]
        vbuf = [RV[:, 0:4096].rearrange("p (t h) -> p t h", h=128),
                RV[:, 4096:8192].rearrange("p (t h) -> p t h", h=128)]
        ktmp = RV[:, 0:1024]
        vtmp = RV[:, 1024:2048].rearrange("p (t h) -> p t h", h=128)
        sqv = [RV[:, 2048:3072], RV[:, 3072:4096]]
        hbf = [RV[:, 6144:6656], RV[:, 6656:7168]]
        wbuf = [WB[:, 0:4096].rearrange("p (k j) -> p k j", j=128),
                WB[:, 4096:8192].rearrange("p (k j) -> p k j", j=128)]
        tri = F1[:, 4096:6144].rearrange("p (d q) -> p d q", q=512)
        pbuf = [WB[:, i * 1024:(i + 1) * 1024] for i in range(6)]
        rs_v = WB[:, 6144:7168]
        rs_p = WB[:, 7168:8192]
        vt32 = F1[:, 0:1024]
        ssum = F1[0:1, 2048:3072]
        hsb2 = F1[:, 3072:4096]
        xstr = [F1[:, 0:1024], F1[:, 1024:2048]]
        ybuf = [F1[:, 2048:3072], F1[:, 3072:4096]]
        tmpc = F1[:, 0:1024]
        tmpr = F1[0:1, 4096:5120]
        abc = [F1[:, 4096:5120], F1[:, 5120:6144]]
        abTa = F1[:, 5120:5128]
        abTb = F1[:, 5128:5136]
        mu = F1[0:1, 6144:7168]
        s2 = F1[0:1, 2048:3072]
        wnf = F1[:, 0:64].rearrange("p (t j) -> p t j", j=8)
        gcol = F1[:, 64:128].rearrange("p (t j) -> p t j", j=8)
        gmx = F1[:, 128:136]
        gsm = F1[:, 136:144]
        gth = F1[:, 144:152]
        gsg = F1[:, 152:160]
        wT = F1[0:8, 1024:2048]
        gt32 = F1[0:9, 3072:4096]
        gtt = F1[:, 4160:4232].rearrange("p (a b c) -> p a b c", b=2, c=9)
        gbf = RV[:, 7168:8192].rearrange("p (a b) -> p a b", b=512)

        PS1 = psa("PS1", [128, 1024])
        PS2 = psa("PS2", [128, 1024])
        PS3 = psa("PS3", [128, 1024])
        PS4 = psa("PS4", [128, 512])
        PS5 = psa("PS5", [128, 512])
        psum_rot = [PS1, PS2]

        sems = {e: es.enter_context(nc.semaphore("s_" + e)) for e in ENG}
        block = es.enter_context(nc.Block())

        def bias(col):
            return cc[:, col:col + 1]

        def bias1(col):
            return cc[0:1, col:col + 1]

        for i, v in enumerate(CVALS):
            T.emit("dve", lambda e, i=i, v=v: e.memset(cc[:, i:i + 1], v),
                   writes=["cc"])
        for key, dst, src in [
            ("eye_b", eyeb_s[:, :], eye_b), ("eye_f", eyef_s[:, :], eye_f),
            ("ones_b", onesb_s[:, :], ones_b), ("onesf", onesf_s[:, :], onesf),
            ("emat", emat_s[:, :], emat), ("bb", bb_s[:, :], bb_p),
            ("bthr", bthr_s[:, :], bthr_bc),
        ]:
            T.emit("sy", lambda e, d=dst, s=src: e.dma_start(out=d, in_=s),
                   writes=[key], dma=True, sem="d_small")
        T.emit("gp", lambda e: e.dma_start(out=wgt_s[:, :], in_=wgt_p),
               writes=["wgt"], dma=True, sem="d_gpm")

        def load_colblock(wpk, j, par):
            T.emit("gp",
                   lambda e, wpk=wpk, j=j, par=par: e.dma_start(
                       out=WB[:, par * 4096:(par + 1) * 4096],
                       in_=wpk[:, j * 4096:(j + 1) * 4096]),
                   writes=[f"wbuf{par}"], dma=True, sem=f"d_wb{par}")

        allx = [f"big1_{k}" for k in range(NCH)]

        def proj_colblock(par, psum, xkeys=None):
            sn = psum.name
            for half in range(2):
                off = half * 512
                for k in range(NCH):
                    T.emit("pe",
                           lambda e, k=k, off=off, par=par, psum=psum:
                           e.matmul(psum[:, off:off + 512], wbuf[par][:, k, :],
                                    BIG1[:, k, off:off + 512],
                                    start=(k == 0), stop=(k == NCH - 1)),
                           reads=[f"wbuf{par}",
                                  f"big1_{k}" if xkeys is None else xkeys],
                           writes=[f"ps_{sn}_{half}"])

        qA = [j for j in range(NCH) if j % 4 < 2]
        qB = [j for j in range(NCH) if j % 4 >= 2]
        qkv_blocks = ([("k", j) for j in range(NKV)] +
                      [("v", j) for j in range(NKV)] +
                      [("q", j) for j in qA + qB])
        wmap = {"k": wkp, "v": wvp, "q": wqp}
        qkv_blocks = qkv_blocks[:QKV_LIMIT]
        if qkv_blocks:
            load_colblock(wmap["k"], 0, 0)

        xTr = xT.rearrange("(k p) t -> p k t", p=128)
        xsems = ["d_gpm", "d_x1", "d_kb1", "d_vb1"]
        for i in range(4):
            lo = i * 8
            T.emit("gp",
                   lambda e, lo=lo: e.dma_start(out=BIG1[:, lo:lo + 8, :],
                                                in_=xTr[:, lo:lo + 8, :]),
                   writes=[f"big1_{j}" for j in range(lo, lo + 8)], dma=True,
                   sem=xsems[i])
        for bi, (kind, j) in enumerate(qkv_blocks):
            par = bi % 2
            if bi + 1 < len(qkv_blocks):
                nk, nj = qkv_blocks[bi + 1]
                load_colblock(wmap[nk], nj, (bi + 1) % 2)
            psum = psum_rot[par]
            sn = psum.name
            proj_colblock(par, psum)
            pk = [f"ps_{sn}_0", f"ps_{sn}_1"]
            if kind == "q":
                qst, qk = (ktmp, "ktmp") if j % 2 == 0 else (
                    RV[:, 2048:3072], "qtmp")
                hl = j % 4
                qdst = cin_qA if hl < 2 else cin_qB
                qrow = (j // 4) * 256 + (hl % 2) * 128
                qck = "cin_qA" if hl < 2 else "cin_qB"
                T.emit("act",
                       lambda e, psum=psum, qst=qst: e.activation(
                           qst, psum[:, :], AF.Copy),
                       reads=pk, writes=[qk])
                T.emit("sy",
                       lambda e, qdst=qdst, qrow=qrow, qst=qst: e.dma_start(
                           out=qdst.ap()[qrow:qrow + 128, :], in_=qst),
                       reads=[qk], writes=[qck], dma=True, sem="d_cink")
            elif kind == "k":
                T.emit("act",
                       lambda e, psum=psum: e.activation(ktmp, psum[:, :], AF.Copy),
                       reads=pk, writes=["ktmp"])
                T.emit("sy",
                       lambda e, j=j: e.dma_start(
                           out=cin_k.ap()[j * 128:(j + 1) * 128, :], in_=ktmp),
                       reads=["ktmp"], writes=["cin_k"], dma=True, sem="d_cink")
            else:
                T.emit("act",
                       lambda e, psum=psum: e.activation(vt32, psum[:, :], AF.Copy),
                       reads=pk, writes=["vt32"])
                for t in range(8):
                    psT = PS4 if t % 2 == 0 else PS5
                    T.emit("pe",
                           lambda e, t=t, psT=psT: e.transpose(
                               psT[:, 0:128], vt32[:, t * 128:(t + 1) * 128],
                               eyef_s[:, :]),
                           reads=["vt32", "eye_f"], writes=[f"pvt{t % 2}"])
                    T.emit("dve",
                           lambda e, t=t, psT=psT: e.tensor_copy(vtmp[:, t, :],
                                                                 psT[:, 0:128]),
                           reads=[f"pvt{t % 2}"], writes=["vtmp"])
                T.emit("sy",
                       lambda e, j=j: e.dma_start(
                           out=cin_v3.ap()[j * 128:(j + 1) * 128, :],
                           in_=RV[:, 1024:2048]),
                       reads=["vtmp"], writes=["cin_v"], dma=True, sem="d_cinv")
            if kind == "k" and j == NKV - 1 and DO_CC:
                T.emit("gp",
                       lambda e: e.collective_compute(
                           "AllToAll", ALU.bypass,
                           replica_groups=[list(range(NCORE))],
                           ins=[cin_k.ap().opt()], outs=[ata_k.ap().opt()]),
                       reads=["cin_k"], writes=["ata_k"], sem="cc", inc=1)
            if kind == "v" and j == NKV - 1 and DO_CC:
                T.emit("gp",
                       lambda e: e.collective_compute(
                           "AllToAll", ALU.bypass,
                           replica_groups=[list(range(NCORE))],
                           ins=[cin_v3.ap().opt()], outs=[ata_v.ap().opt()]),
                       reads=["cin_v"], writes=["ata_v"], sem="cc", inc=1)
            if kind == "q" and j == qA[-1] and DO_CC:
                T.emit("gp",
                       lambda e: e.collective_compute(
                           "AllToAll", ALU.bypass,
                           replica_groups=[list(range(NCORE))],
                           ins=[cin_qA.ap().opt()], outs=[ata_qA.ap().opt()]),
                       reads=["cin_qA"], writes=["ata_qA"], sem="cc", inc=1)

        def ow_loc(g):
            return (g, 0) if g < 8 else (15 - g, 512)

        vview = RV.rearrange("p (a h) -> p a h", h=128)

        T.emit("sy",
               lambda e: e.dma_start(
                   out=RK.rearrange("p (r t) -> p r t", t=TOK),
                   in_=ata_k.ap().rearrange("(r q) t -> q r t", q=HD)),
               reads=["ata_k"], writes=["kres"], dma=True, sem="d_kb0")
        for hh in range(2):
            T.emit("sy",
                   lambda e, hh=hh: e.dma_start(
                       out=BIG2[:, hh * 8:(hh + 1) * 8, :],
                       in_=ata_qA.ap().rearrange("(r q) t -> q r t", q=2 * HD)
                       [hh * 128:(hh + 1) * 128, :, :]),
                   reads=["ata_qA"], writes=[f"qres{hh}"], dma=True,
                   sem="d_x0")

        T.barrier()

        if DO_CC:
            T.emit("gp",
                   lambda e: e.collective_compute(
                       "AllToAll", ALU.bypass,
                       replica_groups=[list(range(NCORE))],
                       ins=[cin_qB.ap().opt()], outs=[ata_qB.ap().opt()]),
                   reads=["cin_qB"], writes=["ata_qB"], sem="cc", inc=1)
        T.emit("sy",
               lambda e: e.dma_start(
                   out=RV.rearrange("p (r t) -> p r t", t=TOK),
                   in_=ata_v.ap().rearrange("(r p) t -> p r t", p=128)),
               reads=["ata_v"], writes=["vres"], dma=True, sem="d_vb0")
        T.emit("sy", lambda e: e.dma_start(out=F1[:, 4096:6144], in_=tri_in),
               writes=["tri"], dma=True, sem="d_small")
        for hh in range(2, 4):
            T.emit("sy",
                   lambda e, hh=hh: e.dma_start(
                       out=BIG2[:, hh * 8:(hh + 1) * 8, :],
                       in_=ata_qB.ap().rearrange("(r q) t -> q r t", q=2 * HD)
                       [(hh - 2) * 128:(hh - 1) * 128, :, :]),
                   reads=["ata_qB"], writes=[f"qres{hh}"], dma=True,
                   sem="d_x0")

        orderA = [(0, g) for g in range(15, -1, -1)]
        orderB = [(1, g) for g in range(15, -1, -1)]
        order = (orderA[:8]
                 + [x for pr in zip(orderA[8:], orderB[:8]) for x in pr]
                 + orderB[8:])
        for gidx, (p2, g) in enumerate(order):
            if True:
                qow, qloc = ow_loc(g)
                qslot = [(2 * p2 + e2) * 8 + qow for e2 in range(2)]
                nsub = 4 * (g + 1)
                del gidx
                av_out = [PS3[:, 0:512], PS3[:, 512:1024]]
                av_key = ["ps_PS3_0", "ps_PS3_1"]
                pending_av = []

                def flush_av(keep):
                    while len(pending_av) > keep:
                        pending_av.pop(0)()

                for si in range(nsub):
                    kc, ki = si // 4, si % 4
                    kow, kloc = ow_loc(kc)
                    kcol = kow * 1024 + kloc + ki * 128
                    va = kow * 8 + kloc // 128 + ki
                    diag = (kc == g)
                    pb = si % 6
                    slot = si % 3
                    first = (si == 0)
                    last = (si == nsub - 1)
                    if slot < 2:
                        psS = psum_rot[slot]
                        souts = [psS[:, 0:512], psS[:, 512:1024]]
                        skeys = [f"ps_{psS.name}_0", f"ps_{psS.name}_1"]
                    else:
                        souts = [PS4[:, :], PS5[:, :]]
                        skeys = ["ps_PS4", "ps_PS5"]
                    for e2 in range(2):
                        T.emit("pe",
                               lambda e, kcol=kcol, so=souts[e2],
                               qs=qslot[e2], qloc=qloc:
                               e.matmul(so,
                                        RK[:, kcol:kcol + 128],
                                        BIG2[:, qs, qloc:qloc + 512],
                                        start=True, stop=True),
                               reads=["kres", f"qres{qslot[e2] // 8}"],
                               writes=[skeys[e2]])
                    if slot < 2:
                        T.emit("act",
                               lambda e, psS=psS, pb=pb: e.activation(
                                   pbuf[pb], psS[:, :], AF.Exp,
                                   bias=bias(C_EXPB), scale=bias(C_SCALE)),
                               reads=skeys + ["cc"],
                               writes=[f"pb{pb}h0", f"pb{pb}h1"])
                    else:
                        for e2 in range(2):
                            T.emit("act",
                                   lambda e, so=souts[e2], pb=pb, e2=e2:
                                   e.activation(
                                       pbuf[pb][:, e2 * 512:e2 * 512 + 512],
                                       so, AF.Exp,
                                       bias=bias(C_EXPB), scale=bias(C_SCALE)),
                                   reads=[skeys[e2], "cc"],
                                   writes=[f"pb{pb}h{e2}"])
                    if diag:
                        for e2 in range(2):
                            off = e2 * 512
                            T.emit("dve",
                                   lambda e, ki=ki, off=off, pb=pb:
                                   e.tensor_tensor(
                                       pbuf[pb][:, off:off + 512],
                                       pbuf[pb][:, off:off + 512],
                                       tri[:, ki, :], ALU.mult),
                                   reads=["tri", f"pb{pb}h{e2}"],
                                   writes=[f"pb{pb}h{e2}"])
                    if first:
                        T.emit("dve",
                               lambda e, pb=pb: e.tensor_copy(rs_v, pbuf[pb]),
                               reads=[f"pb{pb}h0", f"pb{pb}h1"],
                               writes=["rs_v"])
                        T.emit("gp", lambda e: e.memset(rs_p, 0.0),
                               writes=["rs_p"])
                    elif si % 3 == 2:
                        T.emit("gp",
                               lambda e, pb=pb: e.tensor_tensor(
                                   rs_p, rs_p, pbuf[pb], ALU.add),
                               reads=[f"pb{pb}h0", f"pb{pb}h1"],
                               writes=["rs_p"])
                    else:
                        T.emit("dve",
                               lambda e, pb=pb: e.tensor_tensor(
                                   rs_v, rs_v, pbuf[pb], ALU.add),
                               reads=[f"pb{pb}h0", f"pb{pb}h1"],
                               writes=["rs_v"])

                    def mk_av(va=va, pb=pb, first=first, last=last):
                        def emit_av():
                            for e2 in range(2):
                                off = e2 * 512
                                T.emit("pe",
                                       lambda e, va=va, off=off, pb=pb,
                                       first=first, last=last, e2=e2:
                                       e.matmul(av_out[e2],
                                                vview[:, va, :],
                                                pbuf[pb][:, off:off + 512],
                                                start=first, stop=last),
                                       reads=["vres", f"pb{pb}h{e2}"],
                                       writes=[av_key[e2]])
                        return emit_av

                    pending_av.append(mk_av())
                    flush_av(0 if last else 3)

                for e2 in range(2):
                    off = e2 * 512
                    T.emit("pe",
                           lambda e, off=off: e.matmul(
                               PS1[0:1, off:off + 512], onesb_s[:, 0:1],
                               rs_v[:, off:off + 512],
                               start=True, stop=False),
                           reads=["rs_v", "ones_b"], writes=[f"ps_PS1_{e2}"])
                    T.emit("pe",
                           lambda e, off=off: e.matmul(
                               PS1[0:1, off:off + 512], onesb_s[:, 0:1],
                               rs_p[:, off:off + 512],
                               start=False, stop=True),
                           reads=["rs_p", "ones_b"], writes=[f"ps_PS1_{e2}"])
                    T.emit("dve",
                           lambda e, off=off: e.tensor_copy(
                               ssum[:, off:off + 512], PS1[0:1, off:off + 512]),
                           reads=[f"ps_PS1_{e2}"], writes=["ssum"])
                    T.emit("dve",
                           lambda e, off=off: e.reciprocal(
                               ssum[:, off:off + 512],
                               ssum[:, off:off + 512]),
                           reads=["ssum"], writes=["ssum"])
                    T.emit("pe",
                           lambda e, off=off: e.matmul(
                               PS2[:, off:off + 512], onesf_s[0:1, 0:128],
                               ssum[:, off:off + 512],
                               start=True, stop=True),
                           reads=["ssum", "onesf"], writes=[f"ps_PS2_{e2}"])
                    T.emit("dve",
                           lambda e, off=off: e.tensor_copy(
                               hsb2[:, off:off + 512], PS2[:, off:off + 512]),
                           reads=[f"ps_PS2_{e2}"], writes=["hsb"])
                    stq = qslot[e2] - 16 * p2
                    stbuf = BIG1 if p2 == 0 else BIG2
                    stk = f"ao{stq}" if p2 == 0 else f"aoB{stq}"
                    T.emit("dve",
                           lambda e, off=off, e2=e2, stq=stq, qloc=qloc,
                           stbuf=stbuf:
                           e.tensor_tensor(
                               stbuf[:, stq, qloc:qloc + 512],
                               av_out[e2],
                               hsb2[:, off:off + 512], ALU.mult),
                           reads=[av_key[e2], "hsb"],
                           writes=[stk])
            if g != 0:
                continue
            cin_half = cin_aoA if p2 == 0 else cin_aoB
            ata_half = ata_aoA if p2 == 0 else ata_aoB
            ckey = "cin_aoA" if p2 == 0 else "cin_aoB"
            akey = "ata_aoA" if p2 == 0 else "ata_aoB"
            for e2 in range(2):
                hh = 2 * p2 + e2
                src = (BIG1 if p2 == 0 else BIG2)[:, e2 * 8:(e2 + 1) * 8, :]
                pfx = "ao" if p2 == 0 else "aoB"
                T.emit("sy",
                       lambda e, e2=e2, src=src, cin_half=cin_half:
                       e.dma_start(
                           out=cin_half.ap().rearrange("(r q) t -> q r t",
                                                       q=2 * HD)
                           [e2 * 128:(e2 + 1) * 128, :, :],
                           in_=src),
                       reads=[f"{pfx}{e2 * 8 + r}" for r in range(8)],
                       writes=[ckey], dma=True, sem="d_st")
            if DO_CC:
                T.emit("gp",
                       lambda e, cin_half=cin_half, ata_half=ata_half:
                       e.collective_compute(
                           "AllToAll", ALU.bypass,
                           replica_groups=[list(range(NCORE))],
                           ins=[cin_half.ap().opt()],
                           outs=[ata_half.ap().opt()]),
                       reads=[ckey], writes=[akey], sem="cc", inc=1)
            for a in range(2):
                T.emit("sy",
                       lambda e, ata_half=ata_half, p2=p2, a=a: e.dma_start(
                           out=BIG1.rearrange("p (r a) t -> p r a t", a=4)
                           [:, :, 2 * p2 + a, :],
                           in_=ata_half.ap().rearrange("(r q) t -> q r t",
                                                       q=256)
                           [a * 128:(a + 1) * 128, :, :]),
                       reads=[akey],
                       writes=[f"big1_{4 * r + 2 * p2 + a}"
                               for r in range(8)],
                       dma=True, sem="d_vb1" if p2 == 0 else "d_kb1")

        T.barrier()


        def stream_fm_chunk(ap_, c, par, key=None):
            key = key or f"xs{par}"
            T.emit("sy",
                   lambda e, c=c, par=par, ap_=ap_: e.dma_start(
                       out=xstr[par], in_=ap_[c * 128:(c + 1) * 128, :]),
                   writes=[key], dma=True, sem=f"d_x{par}")

        def ln_inline_stats(c, par, stash, skey):
            T.emit("act",
                   lambda e, par=par, stash=stash: e.activation(
                       stash, ybuf[par], AF.Copy),
                   reads=[f"yb{par}"], writes=[skey])
            T.emit("act",
                   lambda e, par=par: e.activation(sqv[par], ybuf[par],
                                                   AF.Square),
                   reads=[f"yb{par}"], writes=[f"sqv{par}"])
            st, sp = (c == 0), (c == NCH - 1)
            for half in range(2):
                off = half * 512
                T.emit("pe",
                       lambda e, off=off, st=st, sp=sp, stash=stash:
                       e.matmul(PS3[0:1, off:off + 512], onesb_s[:, 0:1],
                                stash[:, off:off + 512], start=st, stop=sp),
                       reads=[skey, "ones_b"], writes=[f"ln_sy_{half}"])
                psq = PS5 if half == 0 else PS4
                T.emit("pe",
                       lambda e, par=par, off=off, st=st, sp=sp, psq=psq:
                       e.matmul(psq[0:1, :], onesb_s[:, 0:1],
                                sqv[par][:, off:off + 512], start=st, stop=sp),
                       reads=[f"sqv{par}", "ones_b"], writes=[f"ln_sq_{half}"])

        def ln_rows():
            T.emit("dve", lambda e: e.tensor_scalar(
                mu, PS3[0:1, :], bias1(C_INVD), None, ALU.mult),
                reads=["ln_sy_0", "ln_sy_1", "cc"], writes=["mu"])
            T.emit("dve", lambda e: e.tensor_scalar(
                s2[:, 0:512], PS5[0:1, :], bias1(C_INVD), None, ALU.mult),
                reads=["ln_sq_0", "cc"], writes=["s2"])
            T.emit("dve", lambda e: e.tensor_scalar(
                s2[:, 512:1024], PS4[0:1, :], bias1(C_INVD), None, ALU.mult),
                reads=["ln_sq_1", "cc"], writes=["s2"])
            T.emit("dve", lambda e: e.tensor_tensor(tmpr, mu, mu, ALU.mult),
                   reads=["mu"], writes=["tmpr"])
            T.emit("dve", lambda e: e.tensor_tensor(s2, s2, tmpr, ALU.subtract),
                   reads=["tmpr"], writes=["s2"])
            T.emit("act", lambda e: e.activation(tmpr, s2, AF.Ln, bias=bias1(C_EPS)),
                   reads=["s2", "cc"], writes=["tmpr"])
            T.emit("act", lambda e: e.activation(s2, tmpr, AF.Exp,
                                                 scale=bias1(C_NHALF)),
                   reads=["tmpr", "cc"], writes=["s2"])
            T.emit("dve", lambda e: e.tensor_tensor(mu, mu, s2, ALU.mult),
                   reads=["s2"], writes=["mu"])
            T.emit("dve", lambda e: e.tensor_scalar(mu, mu, bias1(C_M1), None,
                                                    ALU.mult),
                   reads=["cc"], writes=["mu"])

        pend_stats = [None]
        load_colblock(wop, 0, 0)
        stream_fm_chunk(xT, 0, 0)
        T.emit("gp", lambda e: e.dma_start(out=RK[:, 0:4096], in_=wa_p),
               writes=["wa"], dma=True, sem="d_gpm")
        T.emit("gp", lambda e: e.dma_start(out=RK[:, 4096:8192], in_=wbw_p),
               writes=["wbw"], dma=True, sem="d_gpm")
        for c in range(NCH):
            par = c % 2
            if c + 1 < NCH:
                load_colblock(wop, c + 1, (c + 1) % 2)
                stream_fm_chunk(xT, c + 1, (c + 1) % 2)
            psum = psum_rot[par]
            sn = psum.name
            ks = ([k for k in range(NCH) if k % 4 < 2] +
                  [k for k in range(NCH) if k % 4 >= 2])
            for half in range(2):
                off = half * 512
                for k in ks:
                    T.emit("pe",
                           lambda e, k=k, off=off, par=par, psum=psum:
                           e.matmul(psum[:, off:off + 512], wbuf[par][:, k, :],
                                    BIG1[:, k, off:off + 512],
                                    start=(k == ks[0]), stop=(k == ks[-1])),
                           reads=[f"wbuf{par}", f"big1_{k}"],
                           writes=[f"ps_{sn}_{half}"])
            T.emit("dve",
                   lambda e, par=par, psum=psum: e.tensor_tensor(
                       ybuf[par], psum[:, :], xstr[par], ALU.add),
                   reads=[f"ps_{sn}_0", f"ps_{sn}_1", f"xs{par}"],
                   writes=[f"yb{par}"])
            if pend_stats[0] is not None:
                pend_stats[0]()
                pend_stats[0] = None
            pend_stats[0] = (lambda c=c, par=par: ln_inline_stats(
                c, par, BIG2[:, c, :], f"big2_{c}"))
        pend_stats[0]()
        pend_stats[0] = None

        T.barrier()

        ln_rows()
        for idx, src, skey in ((0, s2, "s2"), (1, mu, "mu")):
            for half in range(2):
                off = half * 512
                T.emit("pe",
                       lambda e, src=src, off=off: e.matmul(
                           PS4[:, :], onesf_s[0:1, 0:128],
                           src[:, off:off + 512], start=True, stop=True),
                       reads=[skey, "onesf"], writes=["ps_PS4", "ln_sq_1"])
                T.emit("dve",
                       lambda e, idx=idx, off=off: e.tensor_copy(
                           abc[idx][:, off:off + 512], PS4[:, :]),
                       reads=["ps_PS4"], writes=[f"abc{idx}"])
        for c in range(NCH):
            ln_eng = "gp" if c % 4 == 3 else "dve"
            T.emit(ln_eng,
                   lambda e, c=c: e.tensor_tensor(BIG2[:, c, :], BIG2[:, c, :],
                                                  abc[0], ALU.mult),
                   reads=["abc0"], writes=[f"big2_{c}"])
            T.emit(ln_eng,
                   lambda e, c=c: e.tensor_tensor(BIG2[:, c, :], BIG2[:, c, :],
                                                  abc[1], ALU.add),
                   reads=["abc1"], writes=[f"big2_{c}"])

        T.barrier()

        wgt_v = wgt_s.rearrange("p (k j) -> p k j", j=9)
        x1keys = [f"big2_{k}" for k in range(NCH)]
        for half in range(2):
            off = half * 512
            for k in range(NCH):
                T.emit("pe",
                       lambda e, k=k, off=off: e.matmul(
                           PS3[:, off:off + 512], wa_v[:, k, :],
                           BIG2[:, k, off:off + 512],
                           start=(k == 0), stop=(k == NCH - 1)),
                       reads=["wa", f"big2_{k}"], writes=[f"ps_PS3_{half}"])
            T.emit("act",
                   lambda e, half=half, off=off: e.activation(
                       hbf[half], PS3[:, off:off + 512], AF.Copy),
                   reads=[f"ps_PS3_{half}"], writes=[f"hbf{half}"])
            for k in range(NCH):
                T.emit("pe",
                       lambda e, k=k, off=off: e.matmul(
                           PS4[0:9, :], wgt_v[:, k, :],
                           BIG2[:, k, off:off + 512],
                           start=(k == 0), stop=(k == NCH - 1)),
                       reads=["wgt", f"big2_{k}"], writes=["ps_PS4"])
            T.emit("dve",
                   lambda e, half=half: e.tensor_copy(
                       gt32[:, half * 512:half * 512 + 512], PS4[0:9, :]),
                   reads=["ps_PS4"], writes=["gt32"])

        ge1 = F1[0:8, 0:1024]
        ge2 = F1[0:8, 1024:2048]
        rowD = F1[0:1, 2048:3072]
        rowT = F1[0:1, 4096:5120]
        rowS = F1[0:1, 5120:6144]

        def bc8(src_row, src_key, off, wkey):
            T.emit("pe",
                   lambda e, src_row=src_row, off=off: e.matmul(
                       PS5[0:8, :], onesf_s[0:1, 0:8],
                       src_row[:, off:off + 512], start=True, stop=True),
                   reads=[src_key, "onesf"], writes=[wkey])

        for half in range(2):
            off = half * 512
            T.emit("pe",
                   lambda e, off=off: e.matmul(
                       PS5[0:1, :], eyef_s[0:9, 8:9],
                       gt32[0:9, off:off + 512], start=True, stop=True),
                   reads=["gt32", "eye_f"], writes=["ps_bc"])
            T.emit("dve",
                   lambda e, off=off: e.tensor_copy(rowT[:, off:off + 512],
                                                    PS5[0:1, :]),
                   reads=["ps_bc"], writes=["rowT"])
        T.emit("act", lambda e: e.activation(ge1, gt32[0:8, :], AF.Exp),
               reads=["gt32"], writes=["ge1"])
        for half in range(2):
            off = half * 512
            T.emit("pe",
                   lambda e, off=off: e.matmul(
                       PS4[0:1, :], onesf_s[0:8, 0:1], ge1[:, off:off + 512],
                       start=True, stop=True),
                   reads=["ge1", "onesf"], writes=["ps_den"])
            T.emit("dve",
                   lambda e, off=off: e.tensor_copy(rowD[:, off:off + 512],
                                                    PS4[0:1, :]),
                   reads=["ps_den"], writes=["rowD"])
        T.emit("dve", lambda e: e.reciprocal(rowD, rowD),
               reads=["rowD"], writes=["rowD"])
        T.emit("act", lambda e: e.activation(rowT, rowT, AF.Sigmoid,
                                             bias=bthr_s[0:1, 0:1]),
               reads=["rowT", "bthr"], writes=["rowT"])
        T.emit("dve", lambda e: e.tensor_scalar(rowT, rowT, bias1(C_0125),
                                                None, ALU.mult),
               reads=["cc"], writes=["rowT"])
        for half in range(2):
            off = half * 512
            bc8(rowD, "rowD", off, "ps_bc")
            T.emit("dve",
                   lambda e, off=off: e.tensor_tensor(
                       ge2[:, off:off + 512], ge1[:, off:off + 512],
                       PS5[0:8, :], ALU.mult),
                   reads=["ge1", "ps_bc"], writes=["ge2"])
            bc8(rowT, "rowT", off, "ps_bc")
            T.emit("dve",
                   lambda e, off=off: e.tensor_tensor(
                       ge2[:, off:off + 512], ge2[:, off:off + 512],
                       PS5[0:8, :], ALU.subtract),
                   reads=["ps_bc"], writes=["ge2"])
            T.emit("dve",
                   lambda e, off=off: e.tensor_scalar(
                       ge2[:, off:off + 512], ge2[:, off:off + 512],
                       cc[0:8, C_0:C_0 + 1], None, ALU.max),
                   reads=["cc"], writes=["ge2"])
            T.emit("pe",
                   lambda e, off=off: e.matmul(
                       PS4[0:1, :], onesf_s[0:8, 0:1], ge2[:, off:off + 512],
                       start=True, stop=True),
                   reads=["ge2", "onesf"], writes=["ps_den"])
            T.emit("dve",
                   lambda e, off=off: e.tensor_copy(rowS[:, off:off + 512],
                                                    PS4[0:1, :]),
                   reads=["ps_den"], writes=["rowS"])
        T.emit("act", lambda e: e.activation(rowD, rowS, AF.Sign),
               reads=["rowS"], writes=["rowD"])
        T.emit("dve", lambda e: e.tensor_tensor(rowS, rowS, rowD, ALU.subtract),
               reads=["rowD"], writes=["rowS"])
        T.emit("dve", lambda e: e.tensor_scalar(rowS, rowS, bias1(C_1),
                                                None, ALU.add),
               reads=["cc"], writes=["rowS"])
        T.emit("dve", lambda e: e.reciprocal(rowS, rowS),
               reads=["rowS"], writes=["rowS"])
        T.emit("dve", lambda e: e.tensor_scalar(rowS, rowS, bias1(C_2),
                                                None, ALU.mult),
               reads=["cc"], writes=["rowS"])
        for half in range(2):
            off = half * 512
            bc8(rowS, "rowS", off, "ps_bc")
            T.emit("dve",
                   lambda e, off=off: e.tensor_tensor(
                       ge2[:, off:off + 512], ge2[:, off:off + 512],
                       PS5[0:8, :], ALU.mult),
                   reads=["ps_bc"], writes=["ge2"])
            T.emit("pe",
                   lambda e, off=off: e.matmul(
                       PS4[:, :], emat_s[:, :], ge2[:, off:off + 512],
                       start=True, stop=True),
                   reads=["ge2", "emat"], writes=["ps_den"])
            T.emit("dve",
                   lambda e, half=half: e.tensor_tensor(
                       gbf[:, half, :], hbf[half], PS4[:, :], ALU.mult),
                   reads=["ps_den", f"hbf{half}"], writes=["gbf"])

        T.barrier()

        load_colblock(wbp, 0, 0)
        for c in range(NCH):
            par = c % 2
            if c + 1 < NCH:
                load_colblock(wbp, c + 1, (c + 1) % 2)
            psum = psum_rot[par]
            sn = psum.name
            for half in range(2):
                off = half * 512
                for k in range(NCH):
                    T.emit("pe",
                           lambda e, k=k, off=off, par=par, psum=psum:
                           e.matmul(psum[:, off:off + 512], wbuf[par][:, k, :],
                                    BIG2[:, k, off:off + 512],
                                    start=(k == 0), stop=False),
                           reads=[f"wbuf{par}", f"big2_{k}"],
                           writes=[f"ps_{sn}_{half}"])
                T.emit("pe",
                       lambda e, c=c, off=off, half=half, psum=psum: e.matmul(
                           psum[:, off:off + 512], wbw_v[:, c, :],
                           gbf[:, half, :], start=False, stop=True),
                       reads=["wbw", "gbf"], writes=[f"ps_{sn}_{half}"])
            T.emit("act",
                   lambda e, c=c, par=par, psum=psum: e.activation(
                       ybuf[par], psum[:, :], AF.Identity, bias=bb_s[:, c:c + 1]),
                   reads=[f"ps_{sn}_0", f"ps_{sn}_1", "bb"], writes=[f"yb{par}"])
            T.emit("dve",
                   lambda e, c=c, par=par: e.tensor_tensor(
                       ybuf[par], ybuf[par], BIG2[:, c, :], ALU.add),
                   reads=[f"big2_{c}"], writes=[f"yb{par}"])
            if pend_stats[0] is not None:
                pend_stats[0]()
                pend_stats[0] = None
            pend_stats[0] = (lambda c=c, par=par: ln_inline_stats(
                c, par, BIG1[:, c, :], f"big1_{c}"))
        pend_stats[0]()
        pend_stats[0] = None

        T.barrier()

        ln_rows()
        for tg in range(8):
            for src, skey, dst, dkey in ((s2, "s2", abTa, "abTa"),
                                         (mu, "mu", abTb, "abTb")):
                psq = PS5 if tg % 2 == 0 else PS4
                lk = "ln_sq_0" if tg % 2 == 0 else "ln_sq_1"
                T.emit("pe",
                       lambda e, src=src, tg=tg, psq=psq: e.transpose(
                           psq[:, 0:1], src[:, tg * 128:(tg + 1) * 128],
                           eyef_s[0:1, 0:1]),
                       reads=[skey, "eye_f"], writes=[f"pab{tg % 2}", lk])
                T.emit("dve",
                       lambda e, dst=dst, tg=tg, psq=psq: e.tensor_copy(
                           dst[:, tg:tg + 1], psq[:, 0:1]),
                       reads=[f"pab{tg % 2}"], writes=[dkey])
        pz = [PS3[:, 0:128], PS3[:, 512:640], PS4[:, 0:128], PS5[:, 0:128]]
        pzk = [["ln_sy_0"], ["ln_sy_1"], ["ln_sq_1", "pab1"],
               ["ln_sq_0", "pab0"]]
        tmpc2 = [tmpc, F1[:, 1024:2048]]
        for c in range(NCH):
            tb, tk = tmpc2[c % 2], f"tmpc{c % 2}"
            T.emit("act",
                   lambda e, c=c, tb=tb: e.activation(tb, BIG1[:, c, :],
                                                      AF.Copy),
                   reads=[f"big1_{c}"], writes=[tk])
            for tg in range(8):
                zz = tg % 4
                T.emit("pe",
                       lambda e, tg=tg, zz=zz, tb=tb: e.transpose(
                           pz[zz], tb[:, tg * 128:(tg + 1) * 128],
                           eyef_s[:, :]),
                       reads=[tk, "eye_f"], writes=[f"pz{zz}"] + pzk[zz])
                T.emit("dve",
                       lambda e, c=c, tg=tg, zz=zz: e.tensor_scalar(
                           BIG2[:, tg * 4 + c // 8,
                                (c % 8) * 128:(c % 8) * 128 + 128],
                           pz[zz], abTa[:, tg:tg + 1], abTb[:, tg:tg + 1],
                           ALU.mult, ALU.add),
                       reads=[f"pz{zz}", "abTa", "abTb"],
                       writes=[f"stg{tg}h{c // 16}"])
        for tg in range(8):
            for hf in range(2):
                T.emit("gp",
                       lambda e, tg=tg, hf=hf: e.dma_start(
                           out=out[tg * 128:(tg + 1) * 128,
                                   hf * 2048:(hf + 1) * 2048],
                           in_=BIG2[:, tg * 4 + hf * 2:tg * 4 + hf * 2 + 2, :]
                           .rearrange("p a b -> p (a b)")),
                       reads=[f"stg{tg}h{hf}"], writes=[f"out{tg}"], dma=True,
                       sem="d_st")

        T.emit("gp", lambda e: e.nop(),
               reads=[f"out{tg}" for tg in range(8)], inc=0,
               force=True)

        @block.sync
        def _(sync):
            T.replay("sy", sync, sems)

        @block.tensor
        def _(tensor):
            T.replay("pe", tensor, sems)

        @block.scalar
        def _(scalar):
            T.replay("act", scalar, sems)

        @block.vector
        def _(vector):
            T.replay("dve", vector, sems)

        @block.gpsimd
        def _(gpsimd):
            T.replay("gp", gpsimd, sems)

    return nc


_prog_cache = {}


def _get_prog():
    if "nc" not in _prog_cache:
        _prog_cache["nc"] = build_program()
    return _prog_cache["nc"]


def prepare_in_maps(L0_attn_x, L0_attn_wq, L0_attn_wk, L0_attn_wv, L0_attn_wo,
                    L0_ffn_W_base, L0_ffn_b_base, L0_ffn_W_gate, L0_ffn_W_thr,
                    L0_ffn_b_thr, L0_ffn_A_w, L0_ffn_B_w):
    x = np.asarray(L0_attn_x, np.float32).reshape(S, D)
    wq_ = np.ascontiguousarray(np.asarray(L0_attn_wq, np.float32))
    wk_ = np.ascontiguousarray(np.asarray(L0_attn_wk, np.float32))
    wv_ = np.ascontiguousarray(np.asarray(L0_attn_wv, np.float32))
    wo_ = np.ascontiguousarray(np.asarray(L0_attn_wo, np.float32))
    wb_ = np.ascontiguousarray(np.asarray(L0_ffn_W_base, np.float32).T)
    bb = np.asarray(L0_ffn_b_base, np.float32)
    wg = np.asarray(L0_ffn_W_gate, np.float32)
    wthr = np.asarray(L0_ffn_W_thr, np.float32)
    bthr = np.asarray(L0_ffn_b_thr, np.float32)
    aw = np.asarray(L0_ffn_A_w, np.float32)
    bw = np.asarray(L0_ffn_B_w, np.float32)

    def pack_w(w):
        ncb = w.shape[1] // 128
        return np.ascontiguousarray(
            w.reshape(NCH, 128, ncb, 128).transpose(1, 2, 0, 3)
            .reshape(128, ncb * NCH * 128))

    wa_p = np.ascontiguousarray(
        aw.transpose(2, 0, 1).reshape(D, 128)
        .reshape(NCH, 128, 128).transpose(1, 0, 2).reshape(128, NCH * 128))
    wgtT = np.concatenate([wg.T, wthr[:, None]], axis=1)
    wgt_p = np.ascontiguousarray(
        wgtT.reshape(NCH, 128, 9).transpose(1, 0, 2).reshape(128, NCH * 9))
    wbw_p = np.ascontiguousarray(bw.transpose(0, 2, 1).reshape(128, D))
    bb_p = np.ascontiguousarray(bb.reshape(NCH, 128).T)
    bthr_b = np.ascontiguousarray(np.broadcast_to(bthr.reshape(1, 1), (128, 1)))

    eye = np.eye(128)
    ones512 = np.ones((128, 512))
    emat_np = np.zeros((8, 128), np.float32)
    for e in range(8):
        emat_np[e, e * 16:(e + 1) * 16] = 1.0
    kk = np.arange(128)[:, None]
    qq = np.arange(512)[None, :]
    tri_np = np.concatenate([(qq >= 128 * d + kk).astype(np.float32)
                             for d in range(4)], axis=1)

    in_common = {
        "wqp": pack_w(wq_), "wkp": pack_w(wk_), "wvp": pack_w(wv_),
        "wop": pack_w(wo_), "wbp": pack_w(wb_),
        "wa_p": wa_p, "wgt_p": wgt_p, "wbw_p": wbw_p, "bb_p": bb_p,
        "bthr_bc": bthr_b,
        "eye_b": eye.astype(ml_dtypes.bfloat16),
        "eye_f": eye.astype(np.float32),
        "ones_b": ones512.astype(ml_dtypes.bfloat16),
        "onesf": ones512[:, 0:128].astype(np.float32),
        "emat": emat_np, "tri_in": tri_np.astype(np.float32),
    }

    in_maps = []
    for c in range(NCORE):
        gA, gB = c, 15 - c
        xs = np.concatenate([x[gA * 512:(gA + 1) * 512],
                             x[gB * 512:(gB + 1) * 512]], axis=0)
        xTc = np.ascontiguousarray(xs.T)
        in_maps.append({**in_common, "xT": xTc})
    return in_maps


def assemble_out(results):
    out_full = np.empty((S, D), np.float32)
    for c in range(NCORE):
        o = results[c]["out"]
        gA, gB = c, 15 - c
        out_full[gA * 512:(gA + 1) * 512] = o[0:512]
        out_full[gB * 512:(gB + 1) * 512] = o[512:1024]
    return out_full.reshape(1, S, D)


def kernel(**inputs):
    in_maps = prepare_in_maps(**inputs)
    nc = _get_prog()
    res = run_bass_kernel_spmd(nc, in_maps, core_ids=list(range(NCORE)))
    return assemble_out(res.results)





if __name__ == "__main__":
    nc = build_program()
    n = sum(1 for _ in nc.all_instructions())
    print("instructions:", n)


# version 16 (final)
# speedup vs baseline: 1.0088x; 1.0088x over previous
import contextlib
import os
os.environ["TRN_ALLOW_ICI_ALL_TO_ALL"] = "1"
import numpy as np
import ml_dtypes
import concourse.bass as bass
import concourse.mybir as mybir
from concourse.bass_utils import run_bass_kernel_spmd

F32 = mybir.dt.float32
BF16 = mybir.dt.bfloat16
AF = mybir.ActivationFunctionType
ALU = mybir.AluOpType
AX = mybir.AxisListType

D = 4096
S = 8192
NCORE = 8
TOK = 1024
NCH = 32
NH = 32
NKV = 8
HD = 128
SCALE = HD ** -0.5
EXPB = -20.0
EPS = 1e-5
NEG = -1.0e9

C_EXPB, C_SCALE, C_EPS, C_NHALF, C_INVD, C_M1, C_0125, C_2, C_0, C_1 = range(10)
CVALS = [EXPB, SCALE, EPS, -0.5, 1.0 / D, -1.0, 0.125, 2.0, 0.0, 1.0]

DBG = False
PH_LIMIT = 99
QKV_LIMIT = 99
DO_CC = True

HW_ENG = ("pe", "act", "dve", "gp", "sy")
DMA_SEMS = ("d_kb0", "d_kb1", "d_vb0", "d_vb1", "d_cink", "d_cinv",
            "d_small", "d_x0", "d_x1", "d_st", "d_wb0", "d_wb1",
            "d_gpm")
ENG = HW_ENG + ("cc",) + DMA_SEMS
ASYNC = set(DMA_SEMS) | {"cc"}


class Tracker:

    def __init__(self):
        self.cnt = {e: 0 for e in ENG}
        self.waited = {}
        self.bufs = {}
        self.q = {e: [] for e in HW_ENG}
        self.phase = 0

    def _b(self, key):
        if key not in self.bufs:
            self.bufs[key] = {"w": None, "r": []}
        return self.bufs[key]

    def emit(self, eng, fn, reads=(), writes=(), dma=False, sem=None, inc=None,
             manual_waits=None, force=False):
        if self.phase > PH_LIMIT and not force:
            return None
        sem = sem or eng
        assert (sem in ASYNC) == (dma or sem == "cc"), (eng, sem, dma)
        inc = inc if inc is not None else (16 if dma else 1)
        waits = {}

        def add_wait(dep):
            if dep is None:
                return
            deng, dval = dep
            if deng == eng and eng == "pe":
                return
            if deng in ASYNC and deng != "cc":
                dval = self.cnt[deng]
            if dval <= self.waited.get((eng, deng), 0):
                return
            waits[deng] = max(waits.get(deng, 0), dval)

        for k in reads:
            add_wait(self._b(k)["w"])
        for k in writes:
            b = self._b(k)
            add_wait(b["w"])
            for r in b["r"]:
                add_wait(r)
        if manual_waits:
            for deng, dval in manual_waits.items():
                add_wait((deng, dval))
        self.cnt[sem] += inc
        me = (sem, self.cnt[sem]) if inc else None
        if me is not None:
            for k in reads:
                self._b(k)["r"].append(me)
            for k in writes:
                b = self._b(k)
                b["w"] = me
                b["r"] = []
        for deng, dval in waits.items():
            self.waited[(eng, deng)] = dval
        self.q[eng].append((dict(waits), fn, sem, inc))
        return me

    def barrier(self):
        self.phase += 1
        if self.phase > PH_LIMIT:
            return
        snap = dict(self.cnt)
        for e in HW_ENG:
            mw = {d: snap[d] for d in ENG
                  if snap[d] > 0 and not (d == e and d not in ASYNC)}
            self.emit(e, lambda en: en.nop(), manual_waits=mw, inc=0)
        for b in self.bufs.values():
            b["w"] = None
            b["r"] = []

    def replay(self, eng_name, eng_obj, sems):
        for waits, fn, sem, inc in self.q[eng_name]:
            for deng, dval in waits.items():
                eng_obj.wait_ge(sems[deng], dval)
            inst = fn(eng_obj)
            if inst is not None and inc:
                inst.then_inc(sems[sem], inc)


def build_program():
    nc = bass.Bass("TRN2", target_bir_lowering=False)

    def din(name, shape, dt=F32):
        return nc.dram_tensor(name, shape, dt, kind="ExternalInput").ap()

    xT = din("xT", [D, TOK])
    wqp = din("wqp", [128, NCH * NCH * 128])
    wkp = din("wkp", [128, NKV * NCH * 128])
    wvp = din("wvp", [128, NKV * NCH * 128])
    wop = din("wop", [128, NCH * NCH * 128])
    wbp = din("wbp", [128, NCH * NCH * 128])
    wa_p = din("wa_p", [128, NCH * 128])
    wgt_p = din("wgt_p", [128, NCH * 9])
    wbw_p = din("wbw_p", [128, D])
    bb_p = din("bb_p", [128, NCH])
    bthr_bc = din("bthr_bc", [128, 1])
    eye_b = din("eye_b", [128, 128], BF16)
    eye_f = din("eye_f", [128, 128])
    ones_b = din("ones_b", [128, 512], BF16)
    onesf = din("onesf", [128, 128])
    emat = din("emat", [8, 128])
    tri_in = din("tri_in", [128, 4 * 512])

    out = nc.dram_tensor("out", [TOK, D], F32, kind="ExternalOutput").ap()

    cin_k = nc.dram_tensor("cin_k", [NKV * HD, TOK], BF16, kind="Internal")
    ata_k = nc.dram_tensor("ata_k", [NKV * HD, TOK], BF16, kind="Internal")
    cin_v3 = nc.dram_tensor("cin_v3", [NKV * 128, 8 * HD], BF16, kind="Internal")
    ata_v = nc.dram_tensor("ata_v", [NKV * 128, 8 * HD], BF16, kind="Internal")
    cin_qA = nc.dram_tensor("cin_qA", [NH * HD // 2, TOK], BF16, kind="Internal")
    cin_qB = nc.dram_tensor("cin_qB", [NH * HD // 2, TOK], BF16, kind="Internal")
    ata_qA = nc.dram_tensor("ata_qA", [NH * HD // 2, TOK], BF16, kind="Internal")
    ata_qB = nc.dram_tensor("ata_qB", [NH * HD // 2, TOK], BF16, kind="Internal")
    cin_aoA = nc.dram_tensor("cin_aoA", [NH * HD // 2, TOK], BF16, kind="Internal")
    cin_aoB = nc.dram_tensor("cin_aoB", [NH * HD // 2, TOK], BF16, kind="Internal")
    ata_aoA = nc.dram_tensor("ata_aoA", [NH * HD // 2, TOK], BF16, kind="Internal")
    ata_aoB = nc.dram_tensor("ata_aoB", [NH * HD // 2, TOK], BF16, kind="Internal")

    es = contextlib.ExitStack()
    T = Tracker()

    with es:
        def sb(name, shape, dt):
            return es.enter_context(nc.sbuf_tensor(name, shape, dt))

        def psa(name, shape, dt=F32):
            return es.enter_context(nc.psum_tensor(name, shape, dt))

        BIG1 = sb("BIG1", [128, NCH, TOK], BF16)
        BIG2 = sb("BIG2", [128, NCH, TOK], BF16)
        RK = sb("RK", [128, 8192], BF16)
        RV = sb("RV", [128, 8192], BF16)
        WB = sb("WB", [128, 8192], BF16)
        F1 = sb("F1", [128, 7168], F32)
        cc = sb("cc", [128, 16], F32)
        eyeb_s = sb("eyeb_s", [128, 128], BF16)
        eyef_s = sb("eyef_s", [128, 128], F32)
        onesb_s = sb("onesb_s", [128, 512], BF16)
        onesf_s = sb("onesf_s", [128, 128], F32)
        emat_s = sb("emat_s", [8, 128], F32)
        wgt_s = sb("wgt_s", [128, NCH * 9], BF16)
        bb_s = sb("bb_s", [128, NCH], F32)
        bthr_s = sb("bthr_s", [128, 1], F32)

        kbuf = [RK[:, 0:4096], RK[:, 4096:8192]]
        wa_v = RK[:, 0:4096].rearrange("p (k j) -> p k j", j=128)
        wbw_v = RK[:, 4096:8192].rearrange("p (k j) -> p k j", j=128)
        vb_flat = [RV[:, 0:4096], RV[:, 4096:8192]]
        vbuf = [RV[:, 0:4096].rearrange("p (t h) -> p t h", h=128),
                RV[:, 4096:8192].rearrange("p (t h) -> p t h", h=128)]
        ktmp = RV[:, 0:1024]
        vtmp = RV[:, 1024:2048].rearrange("p (t h) -> p t h", h=128)
        sqv = [RV[:, 2048:3072], RV[:, 3072:4096]]
        hbf = [RV[:, 6144:6656], RV[:, 6656:7168]]
        wbuf = [WB[:, 0:4096].rearrange("p (k j) -> p k j", j=128),
                WB[:, 4096:8192].rearrange("p (k j) -> p k j", j=128)]
        tri = F1[:, 4096:6144].rearrange("p (d q) -> p d q", q=512)
        pbuf = [WB[:, i * 1024:(i + 1) * 1024] for i in range(6)]
        rs_v = WB[:, 6144:7168]
        rs_p = WB[:, 7168:8192]
        vt32 = F1[:, 0:1024]
        ssum = F1[0:1, 2048:3072]
        hsb2 = F1[:, 3072:4096]
        xstr = [F1[:, 0:1024], F1[:, 1024:2048]]
        ybuf = [F1[:, 2048:3072], F1[:, 3072:4096]]
        tmpc = F1[:, 0:1024]
        tmpr = F1[0:1, 4096:5120]
        abc = [F1[:, 4096:5120], F1[:, 5120:6144]]
        abTa = F1[:, 5120:5128]
        abTb = F1[:, 5128:5136]
        mu = F1[0:1, 6144:7168]
        s2 = F1[0:1, 2048:3072]
        wnf = F1[:, 0:64].rearrange("p (t j) -> p t j", j=8)
        gcol = F1[:, 64:128].rearrange("p (t j) -> p t j", j=8)
        gmx = F1[:, 128:136]
        gsm = F1[:, 136:144]
        gth = F1[:, 144:152]
        gsg = F1[:, 152:160]
        wT = F1[0:8, 1024:2048]
        gt32 = F1[0:9, 3072:4096]
        gtt = F1[:, 4160:4232].rearrange("p (a b c) -> p a b c", b=2, c=9)
        gbf = RV[:, 7168:8192].rearrange("p (a b) -> p a b", b=512)

        PS1 = psa("PS1", [128, 1024])
        PS2 = psa("PS2", [128, 1024])
        PS3 = psa("PS3", [128, 1024])
        PS4 = psa("PS4", [128, 512])
        PS5 = psa("PS5", [128, 512])
        psum_rot = [PS1, PS2]

        sems = {e: es.enter_context(nc.semaphore("s_" + e)) for e in ENG}
        block = es.enter_context(nc.Block())

        def bias(col):
            return cc[:, col:col + 1]

        def bias1(col):
            return cc[0:1, col:col + 1]

        for i, v in enumerate(CVALS):
            T.emit("dve", lambda e, i=i, v=v: e.memset(cc[:, i:i + 1], v),
                   writes=["cc"])
        for key, dst, src in [
            ("eye_b", eyeb_s[:, :], eye_b), ("eye_f", eyef_s[:, :], eye_f),
            ("ones_b", onesb_s[:, :], ones_b), ("onesf", onesf_s[:, :], onesf),
            ("emat", emat_s[:, :], emat), ("bb", bb_s[:, :], bb_p),
            ("bthr", bthr_s[:, :], bthr_bc),
        ]:
            T.emit("sy", lambda e, d=dst, s=src: e.dma_start(out=d, in_=s),
                   writes=[key], dma=True, sem="d_small")
        T.emit("gp", lambda e: e.dma_start(out=wgt_s[:, :], in_=wgt_p),
               writes=["wgt"], dma=True, sem="d_gpm")

        def load_colblock(wpk, j, par):
            T.emit("gp",
                   lambda e, wpk=wpk, j=j, par=par: e.dma_start(
                       out=WB[:, par * 4096:(par + 1) * 4096],
                       in_=wpk[:, j * 4096:(j + 1) * 4096]),
                   writes=[f"wbuf{par}"], dma=True, sem=f"d_wb{par}")

        allx = [f"big1_{k}" for k in range(NCH)]

        def proj_colblock(par, psum, xkeys=None):
            sn = psum.name
            for half in range(2):
                off = half * 512
                for k in range(NCH):
                    T.emit("pe",
                           lambda e, k=k, off=off, par=par, psum=psum:
                           e.matmul(psum[:, off:off + 512], wbuf[par][:, k, :],
                                    BIG1[:, k, off:off + 512],
                                    start=(k == 0), stop=(k == NCH - 1)),
                           reads=[f"wbuf{par}",
                                  f"big1_{k}" if xkeys is None else xkeys],
                           writes=[f"ps_{sn}_{half}"])

        qA = [j for j in range(NCH) if j % 4 < 2]
        qB = [j for j in range(NCH) if j % 4 >= 2]
        qkv_blocks = ([("k", j) for j in range(NKV)] +
                      [("v", j) for j in range(NKV)] +
                      [("q", j) for j in qA + qB])
        wmap = {"k": wkp, "v": wvp, "q": wqp}
        qkv_blocks = qkv_blocks[:QKV_LIMIT]
        if qkv_blocks:
            load_colblock(wmap["k"], 0, 0)

        xTr = xT.rearrange("(k p) t -> p k t", p=128)
        xsems = ["d_gpm", "d_x1", "d_kb1", "d_vb1"]
        for i in range(4):
            lo = i * 8
            T.emit("gp",
                   lambda e, lo=lo: e.dma_start(out=BIG1[:, lo:lo + 8, :],
                                                in_=xTr[:, lo:lo + 8, :]),
                   writes=[f"big1_{j}" for j in range(lo, lo + 8)], dma=True,
                   sem=xsems[i])
        for bi, (kind, j) in enumerate(qkv_blocks):
            par = bi % 2
            if bi + 1 < len(qkv_blocks):
                nk, nj = qkv_blocks[bi + 1]
                load_colblock(wmap[nk], nj, (bi + 1) % 2)
            psum = psum_rot[par]
            sn = psum.name
            proj_colblock(par, psum)
            pk = [f"ps_{sn}_0", f"ps_{sn}_1"]
            if kind == "q":
                qst, qk = (ktmp, "ktmp") if j % 2 == 0 else (
                    RV[:, 2048:3072], "qtmp")
                hl = j % 4
                qdst = cin_qA if hl < 2 else cin_qB
                qrow = (j // 4) * 256 + (hl % 2) * 128
                qck = "cin_qA" if hl < 2 else "cin_qB"
                T.emit("act",
                       lambda e, psum=psum, qst=qst: e.activation(
                           qst, psum[:, :], AF.Copy),
                       reads=pk, writes=[qk])
                T.emit("sy",
                       lambda e, qdst=qdst, qrow=qrow, qst=qst: e.dma_start(
                           out=qdst.ap()[qrow:qrow + 128, :], in_=qst),
                       reads=[qk], writes=[qck], dma=True, sem="d_cink")
            elif kind == "k":
                T.emit("act",
                       lambda e, psum=psum: e.activation(ktmp, psum[:, :], AF.Copy),
                       reads=pk, writes=["ktmp"])
                T.emit("sy",
                       lambda e, j=j: e.dma_start(
                           out=cin_k.ap()[j * 128:(j + 1) * 128, :], in_=ktmp),
                       reads=["ktmp"], writes=["cin_k"], dma=True, sem="d_cink")
            else:
                T.emit("act",
                       lambda e, psum=psum: e.activation(vt32, psum[:, :], AF.Copy),
                       reads=pk, writes=["vt32"])
                for t in range(8):
                    psT = PS4 if t % 2 == 0 else PS5
                    T.emit("pe",
                           lambda e, t=t, psT=psT: e.transpose(
                               psT[:, 0:128], vt32[:, t * 128:(t + 1) * 128],
                               eyef_s[:, :]),
                           reads=["vt32", "eye_f"], writes=[f"pvt{t % 2}"])
                    T.emit("dve",
                           lambda e, t=t, psT=psT: e.tensor_copy(vtmp[:, t, :],
                                                                 psT[:, 0:128]),
                           reads=[f"pvt{t % 2}"], writes=["vtmp"])
                T.emit("sy",
                       lambda e, j=j: e.dma_start(
                           out=cin_v3.ap()[j * 128:(j + 1) * 128, :],
                           in_=RV[:, 1024:2048]),
                       reads=["vtmp"], writes=["cin_v"], dma=True, sem="d_cinv")
            if kind == "k" and j == NKV - 1 and DO_CC:
                T.emit("gp",
                       lambda e: e.collective_compute(
                           "AllToAll", ALU.bypass,
                           replica_groups=[list(range(NCORE))],
                           ins=[cin_k.ap().opt()], outs=[ata_k.ap().opt()]),
                       reads=["cin_k"], writes=["ata_k"], sem="cc", inc=1)
            if kind == "v" and j == NKV - 1 and DO_CC:
                T.emit("gp",
                       lambda e: e.collective_compute(
                           "AllToAll", ALU.bypass,
                           replica_groups=[list(range(NCORE))],
                           ins=[cin_v3.ap().opt()], outs=[ata_v.ap().opt()]),
                       reads=["cin_v"], writes=["ata_v"], sem="cc", inc=1)
            if kind == "q" and j == qA[-1] and DO_CC:
                T.emit("gp",
                       lambda e: e.collective_compute(
                           "AllToAll", ALU.bypass,
                           replica_groups=[list(range(NCORE))],
                           ins=[cin_qA.ap().opt()], outs=[ata_qA.ap().opt()]),
                       reads=["cin_qA"], writes=["ata_qA"], sem="cc", inc=1)

        def ow_loc(g):
            return (g, 0) if g < 8 else (15 - g, 512)

        vview = RV.rearrange("p (a h) -> p a h", h=128)

        T.emit("sy",
               lambda e: e.dma_start(
                   out=RK.rearrange("p (r t) -> p r t", t=TOK),
                   in_=ata_k.ap().rearrange("(r q) t -> q r t", q=HD)),
               reads=["ata_k"], writes=["kres"], dma=True, sem="d_kb0")
        for hh in range(2):
            T.emit("sy",
                   lambda e, hh=hh: e.dma_start(
                       out=BIG2[:, hh * 8:(hh + 1) * 8, :],
                       in_=ata_qA.ap().rearrange("(r q) t -> q r t", q=2 * HD)
                       [hh * 128:(hh + 1) * 128, :, :]),
                   reads=["ata_qA"], writes=[f"qres{hh}"], dma=True,
                   sem="d_x0")

        T.barrier()

        if DO_CC:
            T.emit("gp",
                   lambda e: e.collective_compute(
                       "AllToAll", ALU.bypass,
                       replica_groups=[list(range(NCORE))],
                       ins=[cin_qB.ap().opt()], outs=[ata_qB.ap().opt()]),
                   reads=["cin_qB"], writes=["ata_qB"], sem="cc", inc=1)
        T.emit("sy",
               lambda e: e.dma_start(
                   out=RV.rearrange("p (r t) -> p r t", t=TOK),
                   in_=ata_v.ap().rearrange("(r p) t -> p r t", p=128)),
               reads=["ata_v"], writes=["vres"], dma=True, sem="d_vb0")
        T.emit("sy", lambda e: e.dma_start(out=F1[:, 4096:6144], in_=tri_in),
               writes=["tri"], dma=True, sem="d_small")
        for hh in range(2, 4):
            T.emit("sy",
                   lambda e, hh=hh: e.dma_start(
                       out=BIG2[:, hh * 8:(hh + 1) * 8, :],
                       in_=ata_qB.ap().rearrange("(r q) t -> q r t", q=2 * HD)
                       [(hh - 2) * 128:(hh - 1) * 128, :, :]),
                   reads=["ata_qB"], writes=[f"qres{hh}"], dma=True,
                   sem="d_x0")

        orderA = [(0, g) for g in range(15, -1, -1)]
        orderB = [(1, g) for g in range(15, -1, -1)]
        order = (orderA[:8]
                 + [x for pr in zip(orderA[8:], orderB[:8]) for x in pr]
                 + orderB[8:])
        for gidx, (p2, g) in enumerate(order):
            if True:
                qow, qloc = ow_loc(g)
                qslot = [(2 * p2 + e2) * 8 + qow for e2 in range(2)]
                nsub = 4 * (g + 1)
                del gidx
                av_out = [PS3[:, 0:512], PS3[:, 512:1024]]
                av_key = ["ps_PS3_0", "ps_PS3_1"]
                pending_av = []

                def flush_av(keep):
                    while len(pending_av) > keep:
                        pending_av.pop(0)()

                for si in range(nsub):
                    kc, ki = si // 4, si % 4
                    kow, kloc = ow_loc(kc)
                    kcol = kow * 1024 + kloc + ki * 128
                    va = kow * 8 + kloc // 128 + ki
                    diag = (kc == g)
                    pb = si % 6
                    slot = si % 3
                    first = (si == 0)
                    last = (si == nsub - 1)
                    if slot < 2:
                        psS = psum_rot[slot]
                        souts = [psS[:, 0:512], psS[:, 512:1024]]
                        skeys = [f"ps_{psS.name}_0", f"ps_{psS.name}_1"]
                    else:
                        souts = [PS4[:, :], PS5[:, :]]
                        skeys = ["ps_PS4", "ps_PS5"]
                    for e2 in range(2):
                        T.emit("pe",
                               lambda e, kcol=kcol, so=souts[e2],
                               qs=qslot[e2], qloc=qloc:
                               e.matmul(so,
                                        RK[:, kcol:kcol + 128],
                                        BIG2[:, qs, qloc:qloc + 512],
                                        start=True, stop=True),
                               reads=["kres", f"qres{qslot[e2] // 8}"],
                               writes=[skeys[e2]])
                    if slot < 2:
                        T.emit("act",
                               lambda e, psS=psS, pb=pb: e.activation(
                                   pbuf[pb], psS[:, :], AF.Exp,
                                   bias=bias(C_EXPB), scale=bias(C_SCALE)),
                               reads=skeys + ["cc"],
                               writes=[f"pb{pb}h0", f"pb{pb}h1"])
                    else:
                        for e2 in range(2):
                            T.emit("act",
                                   lambda e, so=souts[e2], pb=pb, e2=e2:
                                   e.activation(
                                       pbuf[pb][:, e2 * 512:e2 * 512 + 512],
                                       so, AF.Exp,
                                       bias=bias(C_EXPB), scale=bias(C_SCALE)),
                                   reads=[skeys[e2], "cc"],
                                   writes=[f"pb{pb}h{e2}"])
                    if diag:
                        for e2 in range(2):
                            off = e2 * 512
                            T.emit("dve",
                                   lambda e, ki=ki, off=off, pb=pb:
                                   e.tensor_tensor(
                                       pbuf[pb][:, off:off + 512],
                                       pbuf[pb][:, off:off + 512],
                                       tri[:, ki, :], ALU.mult),
                                   reads=["tri", f"pb{pb}h{e2}"],
                                   writes=[f"pb{pb}h{e2}"])
                    if first:
                        T.emit("dve",
                               lambda e, pb=pb: e.tensor_copy(rs_v, pbuf[pb]),
                               reads=[f"pb{pb}h0", f"pb{pb}h1"],
                               writes=["rs_v"])
                        T.emit("gp", lambda e: e.memset(rs_p, 0.0),
                               writes=["rs_p"])
                    elif si % 3 == 2:
                        T.emit("gp",
                               lambda e, pb=pb: e.tensor_tensor(
                                   rs_p, rs_p, pbuf[pb], ALU.add),
                               reads=[f"pb{pb}h0", f"pb{pb}h1"],
                               writes=["rs_p"])
                    else:
                        T.emit("dve",
                               lambda e, pb=pb: e.tensor_tensor(
                                   rs_v, rs_v, pbuf[pb], ALU.add),
                               reads=[f"pb{pb}h0", f"pb{pb}h1"],
                               writes=["rs_v"])

                    def mk_av(va=va, pb=pb, first=first, last=last):
                        def emit_av():
                            for e2 in range(2):
                                off = e2 * 512
                                T.emit("pe",
                                       lambda e, va=va, off=off, pb=pb,
                                       first=first, last=last, e2=e2:
                                       e.matmul(av_out[e2],
                                                vview[:, va, :],
                                                pbuf[pb][:, off:off + 512],
                                                start=first, stop=last),
                                       reads=["vres", f"pb{pb}h{e2}"],
                                       writes=[av_key[e2]])
                        return emit_av

                    pending_av.append(mk_av())
                    flush_av(0 if last else 3)

                for e2 in range(2):
                    off = e2 * 512
                    T.emit("pe",
                           lambda e, off=off: e.matmul(
                               PS1[0:1, off:off + 512], onesb_s[:, 0:1],
                               rs_v[:, off:off + 512],
                               start=True, stop=False),
                           reads=["rs_v", "ones_b"], writes=[f"ps_PS1_{e2}"])
                    T.emit("pe",
                           lambda e, off=off: e.matmul(
                               PS1[0:1, off:off + 512], onesb_s[:, 0:1],
                               rs_p[:, off:off + 512],
                               start=False, stop=True),
                           reads=["rs_p", "ones_b"], writes=[f"ps_PS1_{e2}"])
                    T.emit("dve",
                           lambda e, off=off: e.tensor_copy(
                               ssum[:, off:off + 512], PS1[0:1, off:off + 512]),
                           reads=[f"ps_PS1_{e2}"], writes=["ssum"])
                    T.emit("dve",
                           lambda e, off=off: e.reciprocal(
                               ssum[:, off:off + 512],
                               ssum[:, off:off + 512]),
                           reads=["ssum"], writes=["ssum"])
                    T.emit("pe",
                           lambda e, off=off: e.matmul(
                               PS2[:, off:off + 512], onesf_s[0:1, 0:128],
                               ssum[:, off:off + 512],
                               start=True, stop=True),
                           reads=["ssum", "onesf"], writes=[f"ps_PS2_{e2}"])
                    T.emit("dve",
                           lambda e, off=off: e.tensor_copy(
                               hsb2[:, off:off + 512], PS2[:, off:off + 512]),
                           reads=[f"ps_PS2_{e2}"], writes=["hsb"])
                    stq = qslot[e2] - 16 * p2
                    stbuf = BIG1 if p2 == 0 else BIG2
                    stk = f"ao{stq}" if p2 == 0 else f"aoB{stq}"
                    T.emit("dve",
                           lambda e, off=off, e2=e2, stq=stq, qloc=qloc,
                           stbuf=stbuf:
                           e.tensor_tensor(
                               stbuf[:, stq, qloc:qloc + 512],
                               av_out[e2],
                               hsb2[:, off:off + 512], ALU.mult),
                           reads=[av_key[e2], "hsb"],
                           writes=[stk])
            if g != 0:
                continue
            cin_half = cin_aoA if p2 == 0 else cin_aoB
            ata_half = ata_aoA if p2 == 0 else ata_aoB
            ckey = "cin_aoA" if p2 == 0 else "cin_aoB"
            akey = "ata_aoA" if p2 == 0 else "ata_aoB"
            for e2 in range(2):
                hh = 2 * p2 + e2
                src = (BIG1 if p2 == 0 else BIG2)[:, e2 * 8:(e2 + 1) * 8, :]
                pfx = "ao" if p2 == 0 else "aoB"
                T.emit("sy",
                       lambda e, e2=e2, src=src, cin_half=cin_half:
                       e.dma_start(
                           out=cin_half.ap().rearrange("(r q) t -> q r t",
                                                       q=2 * HD)
                           [e2 * 128:(e2 + 1) * 128, :, :],
                           in_=src),
                       reads=[f"{pfx}{e2 * 8 + r}" for r in range(8)],
                       writes=[ckey], dma=True, sem="d_st")
            if p2 == 1:
                continue
            if DO_CC:
                T.emit("gp",
                       lambda e, cin_half=cin_half, ata_half=ata_half:
                       e.collective_compute(
                           "AllToAll", ALU.bypass,
                           replica_groups=[list(range(NCORE))],
                           ins=[cin_half.ap().opt()],
                           outs=[ata_half.ap().opt()]),
                       reads=[ckey], writes=[akey], sem="cc", inc=1)
            for a in range(2):
                T.emit("sy",
                       lambda e, ata_half=ata_half, a=a: e.dma_start(
                           out=BIG1.rearrange("p (r a) t -> p r a t", a=4)
                           [:, :, a, :],
                           in_=ata_half.ap().rearrange("(r q) t -> q r t",
                                                       q=256)
                           [a * 128:(a + 1) * 128, :, :]),
                       reads=[akey],
                       writes=[f"big1_{4 * r + a}" for r in range(8)],
                       dma=True, sem="d_vb1")

        T.barrier()

        if DO_CC:
            T.emit("gp",
                   lambda e: e.collective_compute(
                       "AllToAll", ALU.bypass,
                       replica_groups=[list(range(NCORE))],
                       ins=[cin_aoB.ap().opt()], outs=[ata_aoB.ap().opt()]),
                   reads=["cin_aoB"], writes=["ata_aoB"], sem="cc", inc=1)
        for a in range(2):
            T.emit("sy",
                   lambda e, a=a: e.dma_start(
                       out=BIG1.rearrange("p (r a) t -> p r a t", a=4)
                       [:, :, 2 + a, :],
                       in_=ata_aoB.ap().rearrange("(r q) t -> q r t", q=256)
                       [a * 128:(a + 1) * 128, :, :]),
                   reads=["ata_aoB"],
                   writes=[f"big1_{4 * r + 2 + a}" for r in range(8)],
                   dma=True, sem="d_kb1")


        def stream_fm_chunk(ap_, c, par, key=None):
            key = key or f"xs{par}"
            T.emit("sy",
                   lambda e, c=c, par=par, ap_=ap_: e.dma_start(
                       out=xstr[par], in_=ap_[c * 128:(c + 1) * 128, :]),
                   writes=[key], dma=True, sem=f"d_x{par}")

        def ln_inline_stats(c, par, stash, skey):
            T.emit("act",
                   lambda e, par=par, stash=stash: e.activation(
                       stash, ybuf[par], AF.Copy),
                   reads=[f"yb{par}"], writes=[skey])
            T.emit("act",
                   lambda e, par=par: e.activation(sqv[par], ybuf[par],
                                                   AF.Square),
                   reads=[f"yb{par}"], writes=[f"sqv{par}"])
            st, sp = (c == 0), (c == NCH - 1)
            for half in range(2):
                off = half * 512
                T.emit("pe",
                       lambda e, off=off, st=st, sp=sp, stash=stash:
                       e.matmul(PS3[0:1, off:off + 512], onesb_s[:, 0:1],
                                stash[:, off:off + 512], start=st, stop=sp),
                       reads=[skey, "ones_b"], writes=[f"ln_sy_{half}"])
                psq = PS5 if half == 0 else PS4
                T.emit("pe",
                       lambda e, par=par, off=off, st=st, sp=sp, psq=psq:
                       e.matmul(psq[0:1, :], onesb_s[:, 0:1],
                                sqv[par][:, off:off + 512], start=st, stop=sp),
                       reads=[f"sqv{par}", "ones_b"], writes=[f"ln_sq_{half}"])

        def ln_rows():
            T.emit("dve", lambda e: e.tensor_scalar(
                mu, PS3[0:1, :], bias1(C_INVD), None, ALU.mult),
                reads=["ln_sy_0", "ln_sy_1", "cc"], writes=["mu"])
            T.emit("dve", lambda e: e.tensor_scalar(
                s2[:, 0:512], PS5[0:1, :], bias1(C_INVD), None, ALU.mult),
                reads=["ln_sq_0", "cc"], writes=["s2"])
            T.emit("dve", lambda e: e.tensor_scalar(
                s2[:, 512:1024], PS4[0:1, :], bias1(C_INVD), None, ALU.mult),
                reads=["ln_sq_1", "cc"], writes=["s2"])
            T.emit("dve", lambda e: e.tensor_tensor(tmpr, mu, mu, ALU.mult),
                   reads=["mu"], writes=["tmpr"])
            T.emit("dve", lambda e: e.tensor_tensor(s2, s2, tmpr, ALU.subtract),
                   reads=["tmpr"], writes=["s2"])
            T.emit("act", lambda e: e.activation(tmpr, s2, AF.Ln, bias=bias1(C_EPS)),
                   reads=["s2", "cc"], writes=["tmpr"])
            T.emit("act", lambda e: e.activation(s2, tmpr, AF.Exp,
                                                 scale=bias1(C_NHALF)),
                   reads=["tmpr", "cc"], writes=["s2"])
            T.emit("dve", lambda e: e.tensor_tensor(mu, mu, s2, ALU.mult),
                   reads=["s2"], writes=["mu"])
            T.emit("dve", lambda e: e.tensor_scalar(mu, mu, bias1(C_M1), None,
                                                    ALU.mult),
                   reads=["cc"], writes=["mu"])

        pend_stats = [None]
        load_colblock(wop, 0, 0)
        stream_fm_chunk(xT, 0, 0)
        T.emit("gp", lambda e: e.dma_start(out=RK[:, 0:4096], in_=wa_p),
               writes=["wa"], dma=True, sem="d_gpm")
        T.emit("gp", lambda e: e.dma_start(out=RK[:, 4096:8192], in_=wbw_p),
               writes=["wbw"], dma=True, sem="d_gpm")
        for c in range(NCH):
            par = c % 2
            if c + 1 < NCH:
                load_colblock(wop, c + 1, (c + 1) % 2)
                stream_fm_chunk(xT, c + 1, (c + 1) % 2)
            psum = psum_rot[par]
            sn = psum.name
            ks = ([k for k in range(NCH) if k % 4 < 2] +
                  [k for k in range(NCH) if k % 4 >= 2])
            for half in range(2):
                off = half * 512
                for k in ks:
                    T.emit("pe",
                           lambda e, k=k, off=off, par=par, psum=psum:
                           e.matmul(psum[:, off:off + 512], wbuf[par][:, k, :],
                                    BIG1[:, k, off:off + 512],
                                    start=(k == ks[0]), stop=(k == ks[-1])),
                           reads=[f"wbuf{par}", f"big1_{k}"],
                           writes=[f"ps_{sn}_{half}"])
            T.emit("dve",
                   lambda e, par=par, psum=psum: e.tensor_tensor(
                       ybuf[par], psum[:, :], xstr[par], ALU.add),
                   reads=[f"ps_{sn}_0", f"ps_{sn}_1", f"xs{par}"],
                   writes=[f"yb{par}"])
            if pend_stats[0] is not None:
                pend_stats[0]()
                pend_stats[0] = None
            pend_stats[0] = (lambda c=c, par=par: ln_inline_stats(
                c, par, BIG2[:, c, :], f"big2_{c}"))
        pend_stats[0]()
        pend_stats[0] = None

        T.barrier()

        ln_rows()
        for idx, src, skey in ((0, s2, "s2"), (1, mu, "mu")):
            for half in range(2):
                off = half * 512
                T.emit("pe",
                       lambda e, src=src, off=off: e.matmul(
                           PS4[:, :], onesf_s[0:1, 0:128],
                           src[:, off:off + 512], start=True, stop=True),
                       reads=[skey, "onesf"], writes=["ps_PS4", "ln_sq_1"])
                T.emit("dve",
                       lambda e, idx=idx, off=off: e.tensor_copy(
                           abc[idx][:, off:off + 512], PS4[:, :]),
                       reads=["ps_PS4"], writes=[f"abc{idx}"])
        for c in range(NCH):
            ln_eng = "gp" if c % 4 == 3 else "dve"
            T.emit(ln_eng,
                   lambda e, c=c: e.tensor_tensor(BIG2[:, c, :], BIG2[:, c, :],
                                                  abc[0], ALU.mult),
                   reads=["abc0"], writes=[f"big2_{c}"])
            T.emit(ln_eng,
                   lambda e, c=c: e.tensor_tensor(BIG2[:, c, :], BIG2[:, c, :],
                                                  abc[1], ALU.add),
                   reads=["abc1"], writes=[f"big2_{c}"])

        T.barrier()

        wgt_v = wgt_s.rearrange("p (k j) -> p k j", j=9)
        x1keys = [f"big2_{k}" for k in range(NCH)]
        for half in range(2):
            off = half * 512
            for k in range(NCH):
                T.emit("pe",
                       lambda e, k=k, off=off: e.matmul(
                           PS3[:, off:off + 512], wa_v[:, k, :],
                           BIG2[:, k, off:off + 512],
                           start=(k == 0), stop=(k == NCH - 1)),
                       reads=["wa", f"big2_{k}"], writes=[f"ps_PS3_{half}"])
            T.emit("act",
                   lambda e, half=half, off=off: e.activation(
                       hbf[half], PS3[:, off:off + 512], AF.Copy),
                   reads=[f"ps_PS3_{half}"], writes=[f"hbf{half}"])
            for k in range(NCH):
                T.emit("pe",
                       lambda e, k=k, off=off: e.matmul(
                           PS4[0:9, :], wgt_v[:, k, :],
                           BIG2[:, k, off:off + 512],
                           start=(k == 0), stop=(k == NCH - 1)),
                       reads=["wgt", f"big2_{k}"], writes=["ps_PS4"])
            T.emit("dve",
                   lambda e, half=half: e.tensor_copy(
                       gt32[:, half * 512:half * 512 + 512], PS4[0:9, :]),
                   reads=["ps_PS4"], writes=["gt32"])

        ge1 = F1[0:8, 0:1024]
        ge2 = F1[0:8, 1024:2048]
        rowD = F1[0:1, 2048:3072]
        rowT = F1[0:1, 4096:5120]
        rowS = F1[0:1, 5120:6144]

        def bc8(src_row, src_key, off, wkey):
            T.emit("pe",
                   lambda e, src_row=src_row, off=off: e.matmul(
                       PS5[0:8, :], onesf_s[0:1, 0:8],
                       src_row[:, off:off + 512], start=True, stop=True),
                   reads=[src_key, "onesf"], writes=[wkey])

        for half in range(2):
            off = half * 512
            T.emit("pe",
                   lambda e, off=off: e.matmul(
                       PS5[0:1, :], eyef_s[0:9, 8:9],
                       gt32[0:9, off:off + 512], start=True, stop=True),
                   reads=["gt32", "eye_f"], writes=["ps_bc"])
            T.emit("dve",
                   lambda e, off=off: e.tensor_copy(rowT[:, off:off + 512],
                                                    PS5[0:1, :]),
                   reads=["ps_bc"], writes=["rowT"])
        T.emit("act", lambda e: e.activation(ge1, gt32[0:8, :], AF.Exp),
               reads=["gt32"], writes=["ge1"])
        for half in range(2):
            off = half * 512
            T.emit("pe",
                   lambda e, off=off: e.matmul(
                       PS4[0:1, :], onesf_s[0:8, 0:1], ge1[:, off:off + 512],
                       start=True, stop=True),
                   reads=["ge1", "onesf"], writes=["ps_den"])
            T.emit("dve",
                   lambda e, off=off: e.tensor_copy(rowD[:, off:off + 512],
                                                    PS4[0:1, :]),
                   reads=["ps_den"], writes=["rowD"])
        T.emit("dve", lambda e: e.reciprocal(rowD, rowD),
               reads=["rowD"], writes=["rowD"])
        T.emit("act", lambda e: e.activation(rowT, rowT, AF.Sigmoid,
                                             bias=bthr_s[0:1, 0:1]),
               reads=["rowT", "bthr"], writes=["rowT"])
        T.emit("dve", lambda e: e.tensor_scalar(rowT, rowT, bias1(C_0125),
                                                None, ALU.mult),
               reads=["cc"], writes=["rowT"])
        for half in range(2):
            off = half * 512
            bc8(rowD, "rowD", off, "ps_bc")
            T.emit("dve",
                   lambda e, off=off: e.tensor_tensor(
                       ge2[:, off:off + 512], ge1[:, off:off + 512],
                       PS5[0:8, :], ALU.mult),
                   reads=["ge1", "ps_bc"], writes=["ge2"])
            bc8(rowT, "rowT", off, "ps_bc")
            T.emit("dve",
                   lambda e, off=off: e.tensor_tensor(
                       ge2[:, off:off + 512], ge2[:, off:off + 512],
                       PS5[0:8, :], ALU.subtract),
                   reads=["ps_bc"], writes=["ge2"])
            T.emit("dve",
                   lambda e, off=off: e.tensor_scalar(
                       ge2[:, off:off + 512], ge2[:, off:off + 512],
                       cc[0:8, C_0:C_0 + 1], None, ALU.max),
                   reads=["cc"], writes=["ge2"])
            T.emit("pe",
                   lambda e, off=off: e.matmul(
                       PS4[0:1, :], onesf_s[0:8, 0:1], ge2[:, off:off + 512],
                       start=True, stop=True),
                   reads=["ge2", "onesf"], writes=["ps_den"])
            T.emit("dve",
                   lambda e, off=off: e.tensor_copy(rowS[:, off:off + 512],
                                                    PS4[0:1, :]),
                   reads=["ps_den"], writes=["rowS"])
        T.emit("act", lambda e: e.activation(rowD, rowS, AF.Sign),
               reads=["rowS"], writes=["rowD"])
        T.emit("dve", lambda e: e.tensor_tensor(rowS, rowS, rowD, ALU.subtract),
               reads=["rowD"], writes=["rowS"])
        T.emit("dve", lambda e: e.tensor_scalar(rowS, rowS, bias1(C_1),
                                                None, ALU.add),
               reads=["cc"], writes=["rowS"])
        T.emit("dve", lambda e: e.reciprocal(rowS, rowS),
               reads=["rowS"], writes=["rowS"])
        T.emit("dve", lambda e: e.tensor_scalar(rowS, rowS, bias1(C_2),
                                                None, ALU.mult),
               reads=["cc"], writes=["rowS"])
        for half in range(2):
            off = half * 512
            bc8(rowS, "rowS", off, "ps_bc")
            T.emit("dve",
                   lambda e, off=off: e.tensor_tensor(
                       ge2[:, off:off + 512], ge2[:, off:off + 512],
                       PS5[0:8, :], ALU.mult),
                   reads=["ps_bc"], writes=["ge2"])
            T.emit("pe",
                   lambda e, off=off: e.matmul(
                       PS4[:, :], emat_s[:, :], ge2[:, off:off + 512],
                       start=True, stop=True),
                   reads=["ge2", "emat"], writes=["ps_den"])
            T.emit("dve",
                   lambda e, half=half: e.tensor_tensor(
                       gbf[:, half, :], hbf[half], PS4[:, :], ALU.mult),
                   reads=["ps_den", f"hbf{half}"], writes=["gbf"])

        T.barrier()

        load_colblock(wbp, 0, 0)
        for c in range(NCH):
            par = c % 2
            if c + 1 < NCH:
                load_colblock(wbp, c + 1, (c + 1) % 2)
            psum = psum_rot[par]
            sn = psum.name
            for half in range(2):
                off = half * 512
                for k in range(NCH):
                    T.emit("pe",
                           lambda e, k=k, off=off, par=par, psum=psum:
                           e.matmul(psum[:, off:off + 512], wbuf[par][:, k, :],
                                    BIG2[:, k, off:off + 512],
                                    start=(k == 0), stop=False),
                           reads=[f"wbuf{par}", f"big2_{k}"],
                           writes=[f"ps_{sn}_{half}"])
                T.emit("pe",
                       lambda e, c=c, off=off, half=half, psum=psum: e.matmul(
                           psum[:, off:off + 512], wbw_v[:, c, :],
                           gbf[:, half, :], start=False, stop=True),
                       reads=["wbw", "gbf"], writes=[f"ps_{sn}_{half}"])
            T.emit("act",
                   lambda e, c=c, par=par, psum=psum: e.activation(
                       ybuf[par], psum[:, :], AF.Identity, bias=bb_s[:, c:c + 1]),
                   reads=[f"ps_{sn}_0", f"ps_{sn}_1", "bb"], writes=[f"yb{par}"])
            T.emit("dve",
                   lambda e, c=c, par=par: e.tensor_tensor(
                       ybuf[par], ybuf[par], BIG2[:, c, :], ALU.add),
                   reads=[f"big2_{c}"], writes=[f"yb{par}"])
            if pend_stats[0] is not None:
                pend_stats[0]()
                pend_stats[0] = None
            pend_stats[0] = (lambda c=c, par=par: ln_inline_stats(
                c, par, BIG1[:, c, :], f"big1_{c}"))
        pend_stats[0]()
        pend_stats[0] = None

        T.barrier()

        ln_rows()
        for tg in range(8):
            for src, skey, dst, dkey in ((s2, "s2", abTa, "abTa"),
                                         (mu, "mu", abTb, "abTb")):
                psq = PS5 if tg % 2 == 0 else PS4
                lk = "ln_sq_0" if tg % 2 == 0 else "ln_sq_1"
                T.emit("pe",
                       lambda e, src=src, tg=tg, psq=psq: e.transpose(
                           psq[:, 0:1], src[:, tg * 128:(tg + 1) * 128],
                           eyef_s[0:1, 0:1]),
                       reads=[skey, "eye_f"], writes=[f"pab{tg % 2}", lk])
                T.emit("dve",
                       lambda e, dst=dst, tg=tg, psq=psq: e.tensor_copy(
                           dst[:, tg:tg + 1], psq[:, 0:1]),
                       reads=[f"pab{tg % 2}"], writes=[dkey])
        pz = [PS3[:, 0:128], PS3[:, 512:640], PS4[:, 0:128], PS5[:, 0:128]]
        pzk = [["ln_sy_0"], ["ln_sy_1"], ["ln_sq_1", "pab1"],
               ["ln_sq_0", "pab0"]]
        tmpc2 = [tmpc, F1[:, 1024:2048]]
        for c in range(NCH):
            tb, tk = tmpc2[c % 2], f"tmpc{c % 2}"
            T.emit("act",
                   lambda e, c=c, tb=tb: e.activation(tb, BIG1[:, c, :],
                                                      AF.Copy),
                   reads=[f"big1_{c}"], writes=[tk])
            for tg in range(8):
                zz = tg % 4
                T.emit("pe",
                       lambda e, tg=tg, zz=zz, tb=tb: e.transpose(
                           pz[zz], tb[:, tg * 128:(tg + 1) * 128],
                           eyef_s[:, :]),
                       reads=[tk, "eye_f"], writes=[f"pz{zz}"] + pzk[zz])
                T.emit("dve",
                       lambda e, c=c, tg=tg, zz=zz: e.tensor_scalar(
                           BIG2[:, tg * 4 + c // 8,
                                (c % 8) * 128:(c % 8) * 128 + 128],
                           pz[zz], abTa[:, tg:tg + 1], abTb[:, tg:tg + 1],
                           ALU.mult, ALU.add),
                       reads=[f"pz{zz}", "abTa", "abTb"],
                       writes=[f"stg{tg}h{c // 16}"])
        for tg in range(8):
            for hf in range(2):
                T.emit("gp",
                       lambda e, tg=tg, hf=hf: e.dma_start(
                           out=out[tg * 128:(tg + 1) * 128,
                                   hf * 2048:(hf + 1) * 2048],
                           in_=BIG2[:, tg * 4 + hf * 2:tg * 4 + hf * 2 + 2, :]
                           .rearrange("p a b -> p (a b)")),
                       reads=[f"stg{tg}h{hf}"], writes=[f"out{tg}"], dma=True,
                       sem="d_st")

        T.emit("gp", lambda e: e.nop(),
               reads=[f"out{tg}" for tg in range(8)], inc=0,
               force=True)

        @block.sync
        def _(sync):
            T.replay("sy", sync, sems)

        @block.tensor
        def _(tensor):
            T.replay("pe", tensor, sems)

        @block.scalar
        def _(scalar):
            T.replay("act", scalar, sems)

        @block.vector
        def _(vector):
            T.replay("dve", vector, sems)

        @block.gpsimd
        def _(gpsimd):
            T.replay("gp", gpsimd, sems)

    return nc


_prog_cache = {}


def _get_prog():
    if "nc" not in _prog_cache:
        _prog_cache["nc"] = build_program()
    return _prog_cache["nc"]


def prepare_in_maps(L0_attn_x, L0_attn_wq, L0_attn_wk, L0_attn_wv, L0_attn_wo,
                    L0_ffn_W_base, L0_ffn_b_base, L0_ffn_W_gate, L0_ffn_W_thr,
                    L0_ffn_b_thr, L0_ffn_A_w, L0_ffn_B_w):
    x = np.asarray(L0_attn_x, np.float32).reshape(S, D)
    wq_ = np.ascontiguousarray(np.asarray(L0_attn_wq, np.float32))
    wk_ = np.ascontiguousarray(np.asarray(L0_attn_wk, np.float32))
    wv_ = np.ascontiguousarray(np.asarray(L0_attn_wv, np.float32))
    wo_ = np.ascontiguousarray(np.asarray(L0_attn_wo, np.float32))
    wb_ = np.ascontiguousarray(np.asarray(L0_ffn_W_base, np.float32).T)
    bb = np.asarray(L0_ffn_b_base, np.float32)
    wg = np.asarray(L0_ffn_W_gate, np.float32)
    wthr = np.asarray(L0_ffn_W_thr, np.float32)
    bthr = np.asarray(L0_ffn_b_thr, np.float32)
    aw = np.asarray(L0_ffn_A_w, np.float32)
    bw = np.asarray(L0_ffn_B_w, np.float32)

    def pack_w(w):
        ncb = w.shape[1] // 128
        return np.ascontiguousarray(
            w.reshape(NCH, 128, ncb, 128).transpose(1, 2, 0, 3)
            .reshape(128, ncb * NCH * 128))

    wa_p = np.ascontiguousarray(
        aw.transpose(2, 0, 1).reshape(D, 128)
        .reshape(NCH, 128, 128).transpose(1, 0, 2).reshape(128, NCH * 128))
    wgtT = np.concatenate([wg.T, wthr[:, None]], axis=1)
    wgt_p = np.ascontiguousarray(
        wgtT.reshape(NCH, 128, 9).transpose(1, 0, 2).reshape(128, NCH * 9))
    wbw_p = np.ascontiguousarray(bw.transpose(0, 2, 1).reshape(128, D))
    bb_p = np.ascontiguousarray(bb.reshape(NCH, 128).T)
    bthr_b = np.ascontiguousarray(np.broadcast_to(bthr.reshape(1, 1), (128, 1)))

    eye = np.eye(128)
    ones512 = np.ones((128, 512))
    emat_np = np.zeros((8, 128), np.float32)
    for e in range(8):
        emat_np[e, e * 16:(e + 1) * 16] = 1.0
    kk = np.arange(128)[:, None]
    qq = np.arange(512)[None, :]
    tri_np = np.concatenate([(qq >= 128 * d + kk).astype(np.float32)
                             for d in range(4)], axis=1)

    in_common = {
        "wqp": pack_w(wq_), "wkp": pack_w(wk_), "wvp": pack_w(wv_),
        "wop": pack_w(wo_), "wbp": pack_w(wb_),
        "wa_p": wa_p, "wgt_p": wgt_p, "wbw_p": wbw_p, "bb_p": bb_p,
        "bthr_bc": bthr_b,
        "eye_b": eye.astype(ml_dtypes.bfloat16),
        "eye_f": eye.astype(np.float32),
        "ones_b": ones512.astype(ml_dtypes.bfloat16),
        "onesf": ones512[:, 0:128].astype(np.float32),
        "emat": emat_np, "tri_in": tri_np.astype(np.float32),
    }

    in_maps = []
    for c in range(NCORE):
        gA, gB = c, 15 - c
        xs = np.concatenate([x[gA * 512:(gA + 1) * 512],
                             x[gB * 512:(gB + 1) * 512]], axis=0)
        xTc = np.ascontiguousarray(xs.T)
        in_maps.append({**in_common, "xT": xTc})
    return in_maps


def assemble_out(results):
    out_full = np.empty((S, D), np.float32)
    for c in range(NCORE):
        o = results[c]["out"]
        gA, gB = c, 15 - c
        out_full[gA * 512:(gA + 1) * 512] = o[0:512]
        out_full[gB * 512:(gB + 1) * 512] = o[512:1024]
    return out_full.reshape(1, S, D)


def kernel(**inputs):
    in_maps = prepare_in_maps(**inputs)
    nc = _get_prog()
    res = run_bass_kernel_spmd(nc, in_maps, core_ids=list(range(NCORE)))
    return assemble_out(res.results)





if __name__ == "__main__":
    nc = build_program()
    n = sum(1 for _ in nc.all_instructions())
    print("instructions:", n)
